# version 01
# baseline (speedup 1.0000x reference)
import math
from contextlib import ExitStack

import numpy as np
import ml_dtypes

import concourse.bass as bass
import concourse.mybir as mybir
from concourse.bass_utils import run_bass_kernel_spmd

F32 = mybir.dt.float32
BF16 = mybir.dt.bfloat16
AF = mybir.ActivationFunctionType
ALU = mybir.AluOpType

BF = ml_dtypes.bfloat16

B = 2
S = 8192
D = 2048
NCORE = 8
CHUNK = 1024
NCH = S // CHUNK
TOK = 2 * CHUNK
H_A = 16
HD = 128
HKV = 4
KVD = HKV * HD
RANK = 16
ROPE_DIMS = 64
ROPE_BASE = 10000.0
N_CTRL = 130156
INTER = 8192
NW = INTER * D
LN_EPS = 1e-5
RMS_EPS = float(np.finfo(np.float32).eps)
DT_PER = D // 128
IT_PER = INTER // 128
TB = 512
NTB = TOK // TB

WTILES = 128
SH_SAMP = NW // NCORE
SH_ROWS = SH_SAMP // 128
GATE_IB = SH_SAMP // D
DOWN_HB = SH_SAMP // INTER


_cache = {}


def spline_tables():
    if "sp" not in _cache:
        t = np.linspace(0.0, N_CTRL - 1.0, NW, dtype=np.float64)
        i = np.clip(np.floor(t).astype(np.int64), 0, N_CTRL - 2)
        f = (t - i).astype(np.float32)
        i_r = i.reshape(-1, 128)
        f_r = f.reshape(-1, 128)
        i0 = i_r[:, 0]
        d = i_r - i0[:, None]
        assert d.min() >= 0 and d.max() <= 1
        nostep = d.max(axis=1) == 0
        cb = np.where(nostep, 128, np.argmax(d == 1, axis=1)).astype(np.float32)
        _cache["sp"] = (i0, cb, f_r[:, 0])
    return _cache["sp"]


def host_constants(core):
    i0, cb, f0 = spline_tables()
    r0 = core * SH_ROWS
    f0_t = np.empty((3, 128, WTILES), np.float32)
    cb_t = np.empty((3, 128, WTILES), np.float32)
    for w in range(3):
        f0_t[w] = f0[r0:r0 + SH_ROWS].reshape(WTILES, 128).T
        cb_t[w] = cb[r0:r0 + SH_ROWS].reshape(WTILES, 128).T
    delta32 = np.float32((N_CTRL - 1.0) / (NW - 1.0))
    cdelta = np.broadcast_to(
        (np.arange(128, dtype=np.float32) * delta32)[None, :], (128, 128)).copy()
    iota_c = np.broadcast_to(
        np.arange(128, dtype=np.float32)[None, :], (128, 128)).copy()
    return dict(f0cb=np.stack([f0_t, cb_t], axis=1),
                cdelta=cdelta, iota=iota_c,
                ident=np.eye(128, dtype=np.float32),
                ones_row=np.ones((1, 128), np.float32),
                invd_col=np.full((128, 1), 1.0 / D, np.float32))


def host_gwin(core, cps):
    i0, _, _ = spline_tables()
    r0 = core * SH_ROWS
    out = np.empty((3, 128, WTILES * 5), np.float32)
    idx = i0[r0:r0 + SH_ROWS].reshape(WTILES, 128)
    for w, cp in enumerate(cps):
        cp_pad = np.concatenate([cp[:1], cp, cp[-1:], cp[-1:], cp[-1:]])
        g = cp_pad[idx[:, :, None] + np.arange(5)[None, None, :]]
        out[w] = np.transpose(g, (1, 0, 2)).reshape(128, WTILES * 5)
    return out


def host_consts_vals():
    v = np.zeros((128, 8), np.float32)
    v[:, 1] = LN_EPS
    v[:, 2] = RMS_EPS
    v[:, 3] = -20.0
    v[:, 4] = 1e-30
    return v


def rope_tables(pos):
    inv = 1.0 / ROPE_BASE ** (np.arange(0, ROPE_DIMS, 2, dtype=np.float32) / ROPE_DIMS)
    fr = pos[None, :].astype(np.float32) * inv[:, None]
    cos = np.cos(fr).astype(np.float32)
    sin = np.sin(fr).astype(np.float32)
    T = pos.shape[0]
    cosT = np.ones((128, T), np.float32)
    sinT = np.zeros((128, T), np.float32)
    cosT[0:32] = cos
    cosT[32:64] = cos
    sinT[0:32] = sin
    sinT[32:64] = sin
    return cosT, sinT


def rope_J():
    J = np.zeros((128, 128), np.float32)
    for i in range(32):
        J[i, 32 + i] = 1.0
        J[32 + i, i] = -1.0
    return np.ascontiguousarray(J.T)


def host_diag_masks():
    m = np.zeros((128, 8, 2, 512), np.float32)
    for s in range(8):
        tk = s * 128 + np.arange(128)
        for h2 in range(2):
            tq = h2 * 512 + np.arange(512)
            m[:, s, h2, :] = (tk[:, None] <= tq[None, :]).astype(np.float32)
    return m


def core_chunks(core):
    g = core % 4
    return g, 7 - g


def core_positions(core):
    cA, cB = core_chunks(core)
    return np.concatenate([np.arange(cA * CHUNK, (cA + 1) * CHUNK),
                           np.arange(cB * CHUNK, (cB + 1) * CHUNK)])


ENGINES = ("tensor", "vector", "scalar", "sync", "gpsimd")


class Ch:
    __slots__ = ("h", "n", "name")

    def __init__(self, handle, name):
        self.h = handle
        self.n = 0
        self.name = name


class Prog:
    def __init__(self, nc, ctx):
        self.nc = nc
        self.ctx = ctx
        self.ops = {e: [] for e in ENGINES}
        self.chs = {}
        self.dma_pend = {e: [] for e in ENGINES}
        self.last_wait = {e: {} for e in ENGINES}

    def ch(self, name):
        if name not in self.chs:
            self.chs[name] = Ch(self.ctx.enter_context(self.nc.semaphore(name)), name)
        return self.chs[name]

    SELF_CH = {"vector": "dve", "scalar": "act", "gpsimd": "gp"}

    def op(self, eng, fn, inc=None, k=1):
        sname = self.SELF_CH.get(eng)
        if sname is not None:
            sch = self.ch(sname)
            if sch.n:
                self.wait(eng, sch, sch.n)
            if inc is None:
                inc = sch
        if inc is None:
            self.ops[eng].append(fn)
            return None
        inc.n += k
        n = inc.n

        def run(e, fn=fn, inc=inc, k=k):
            fn(e).then_inc(inc.h, k)
        self.ops[eng].append(run)
        return n

    def dma_op(self, eng, fn, inc, k=16):
        inc.n += k
        n = inc.n

        def run(e, fn=fn, inc=inc, k=k):
            fn(e).then_inc(inc.h, k)
        self.ops[eng].append(run)
        return n

    def wait(self, eng, ch, n):
        assert n <= ch.n, f"wait {ch.name} {n} > emitted {ch.n}"
        if self.last_wait[eng].get(ch.name, -1) >= n:
            return
        self.last_wait[eng][ch.name] = n
        self.ops[eng].append(lambda e, ch=ch, n=n: e.wait_ge(ch.h, n))

    def dma(self, eng, ch, out, in_):
        return self.dma_op(eng, lambda e, o=out, i=in_: e.dma_start(out=o, in_=i), ch, k=16)

    def finalize(self, block):
        for eng in ENGINES:
            ops = self.ops[eng]

            def body(e, ops=ops):
                for fn in ops:
                    fn(e)
            getattr(block, eng)(body)



def emit_layernorm(p, resid_ap, ln_out_ap, C, tb_n=TB):
    pe, dve, act = p.ch("pe"), p.ch("dve"), p.ch("act")
    xc, xq = C["xcast"], C["xsq"]
    psA, bcA = C["psA"], C["bcA"]
    rows = C["rows4"]

    for dt in range(DT_PER):
        p.op("vector", lambda e, d=dt: e.tensor_copy(xc[:, d, :tb_n], resid_ap[:, d, :]), inc=dve)
    cast_done = dve.n
    for dt in range(DT_PER):
        p.op("vector", lambda e, d=dt: e.tensor_tensor(xq[:, d, :tb_n], xc[:, d, :tb_n], xc[:, d, :tb_n], ALU.mult), inc=dve)
    sq_done = dve.n

    p.wait("tensor", dve, cast_done)
    for dt in range(DT_PER):
        p.op("tensor", lambda e, d=dt: e.matmul(psA[0:1, 0:tb_n], C["invd_col"], xc[:, d, :tb_n],
                                                start=(d == 0), stop=(d == DT_PER - 1)),
             inc=(pe if dt == DT_PER - 1 else None))
    p.wait("tensor", dve, sq_done)
    for dt in range(DT_PER):
        p.op("tensor", lambda e, d=dt: e.matmul(psA[0:1, TB:TB + tb_n], C["invd_col"], xq[:, d, :tb_n],
                                                start=(d == 0), stop=(d == DT_PER - 1)),
             inc=(pe if dt == DT_PER - 1 else None))
    pe_stats = pe.n

    p.wait("scalar", pe, pe_stats)
    p.op("scalar", lambda e: e.copy(rows[:, 0, :tb_n], psA[0:1, 0:tb_n]), inc=act)
    p.wait("vector", act, act.n)
    p.op("vector", lambda e: e.tensor_tensor(rows[:, 1, :tb_n], rows[:, 0, :tb_n], rows[:, 0, :tb_n], ALU.mult), inc=dve)
    p.op("vector", lambda e: e.tensor_tensor(rows[:, 1, :tb_n], psA[0:1, TB:TB + tb_n], rows[:, 1, :tb_n], ALU.subtract), inc=dve)
    p.wait("scalar", dve, dve.n)
    p.op("scalar", lambda e: e.activation(rows[:, 2, :tb_n], rows[:, 1, :tb_n], AF.Ln, bias=C["eps1"], scale=1.0), inc=act)
    p.op("scalar", lambda e: e.activation(rows[:, 2, :tb_n], rows[:, 2, :tb_n], AF.Exp, bias=C["zero1"], scale=-0.5), inc=act)
    rstd_done = act.n

    p.wait("tensor", act, rstd_done)
    p.op("tensor", lambda e: e.matmul(bcA[:, 0, :tb_n], C["ones_f32"], rows[:, 0, :tb_n], start=True, stop=True), inc=pe)
    p.op("tensor", lambda e: e.matmul(bcA[:, 1, :tb_n], C["ones_f32"], rows[:, 2, :tb_n], start=True, stop=True), inc=pe)
    pe_bc = pe.n

    p.wait("vector", pe, pe_bc)
    for dt in range(DT_PER):
        p.op("vector", lambda e, d=dt: e.tensor_tensor(C["ln_t"][:, :tb_n], resid_ap[:, d, :], bcA[:, 0, :tb_n], ALU.subtract), inc=dve)
        p.op("vector", lambda e, d=dt: e.tensor_tensor(ln_out_ap[:, d, :], C["ln_t"][:, :tb_n], bcA[:, 1, :tb_n], ALU.mult), inc=dve)
    return dve.n


def emit_recon_weight(p, T, w, nout_ap, kind, stag_ch, state):
    pe, dve = p.ch("pe"), p.ch("dve")
    g5 = T["g_all"][:, w, :].rearrange("p (t k) -> p t k", k=5)
    coef, cs = T["coef"], T["cscr"]
    f0_col = T["f0cb_s"][:, w, 0, :]
    cb_col = T["f0cb_s"][:, w, 1, :]

    def gk(k):
        return g5[:, :, k]

    def cf(i):
        return coef[:, i, :]

    steps = [
        lambda e: e.tensor_copy(cf(0), gk(1)),
        lambda e: e.tensor_tensor(cs[:, 0, :], gk(2), gk(0), ALU.subtract),
        lambda e: e.tensor_scalar(cf(1), cs[:, 0, :], 0.5, None, ALU.mult),
        lambda e: e.scalar_tensor_tensor(cs[:, 0, :], gk(1), -2.5, gk(0), ALU.mult, ALU.add),
        lambda e: e.scalar_tensor_tensor(cs[:, 0, :], gk(2), 2.0, cs[:, 0, :], ALU.mult, ALU.add),
        lambda e: e.scalar_tensor_tensor(cf(2), gk(3), -0.5, cs[:, 0, :], ALU.mult, ALU.add),
        lambda e: e.tensor_scalar(cs[:, 0, :], gk(0), -0.5, None, ALU.mult),
        lambda e: e.scalar_tensor_tensor(cs[:, 0, :], gk(1), 1.5, cs[:, 0, :], ALU.mult, ALU.add),
        lambda e: e.scalar_tensor_tensor(cs[:, 0, :], gk(2), -1.5, cs[:, 0, :], ALU.mult, ALU.add),
        lambda e: e.scalar_tensor_tensor(cf(3), gk(3), 0.5, cs[:, 0, :], ALU.mult, ALU.add),
        lambda e: e.tensor_copy(cf(4), gk(2)),
        lambda e: e.tensor_tensor(cs[:, 0, :], gk(3), gk(1), ALU.subtract),
        lambda e: e.tensor_scalar(cf(5), cs[:, 0, :], 0.5, None, ALU.mult),
        lambda e: e.scalar_tensor_tensor(cs[:, 0, :], gk(2), -2.5, gk(1), ALU.mult, ALU.add),
        lambda e: e.scalar_tensor_tensor(cs[:, 0, :], gk(3), 2.0, cs[:, 0, :], ALU.mult, ALU.add),
        lambda e: e.scalar_tensor_tensor(cf(6), gk(4), -0.5, cs[:, 0, :], ALU.mult, ALU.add),
        lambda e: e.tensor_scalar(cs[:, 0, :], gk(1), -0.5, None, ALU.mult),
        lambda e: e.scalar_tensor_tensor(cs[:, 0, :], gk(2), 1.5, cs[:, 0, :], ALU.mult, ALU.add),
        lambda e: e.scalar_tensor_tensor(cs[:, 0, :], gk(3), -1.5, cs[:, 0, :], ALU.mult, ALU.add),
        lambda e: e.scalar_tensor_tensor(cf(7), gk(4), 0.5, cs[:, 0, :], ALU.mult, ALU.add),
    ]
    steps += [
        (lambda e, i=i: e.tensor_tensor(cf(4 + i), cf(4 + i), cf(i), ALU.subtract))
        for i in range(4)
    ]
    for st in steps[:-1]:
        p.op("vector", st)
    p.op("vector", steps[-1], inc=dve)

    mt, ft, cpt, ht = T["mt"], T["ft"], T["cpt"], T["ht"]
    wtile, stag = T["wtile"], T["stag"]
    tr_ps = T["tr_ps"]
    GROUP = 16 if kind != "down" else 64
    tr_free = state.setdefault("tr_free", [0, 0, 0, 0])
    wt_free = state.setdefault("wt_free", [0, 0])
    stag_free = state.setdefault("stag_free", [0, 0])
    last_stg = 0
    for t in range(WTILES):
        gidx, tloc = divmod(t, GROUP)
        wsl = t % 2
        tsl = t % 4
        p.op("vector", lambda e, t=t: e.tensor_scalar(mt[:, :], T["iota_c"][:, :], cb_col[:, t:t + 1], None, ALU.is_ge))
        p.op("vector", lambda e, t=t: e.scalar_tensor_tensor(ft[:, :], T["cdelta"][:, :], f0_col[:, t:t + 1], mt[:, :], ALU.add, ALU.subtract))
        for i in range(4):
            p.op("vector", lambda e, t=t, i=i: e.tensor_scalar(cpt[:, i, :], mt[:, :], coef[:, 4 + i, t:t + 1], coef[:, i, t:t + 1], ALU.mult, ALU.add))
        p.op("vector", lambda e: e.tensor_tensor(ht[:, 0, :], cpt[:, 3, :], ft[:, :], ALU.mult))
        p.op("vector", lambda e: e.tensor_tensor(ht[:, 1, :], ht[:, 0, :], cpt[:, 2, :], ALU.add))
        p.op("vector", lambda e: e.tensor_tensor(ht[:, 0, :], ht[:, 1, :], ft[:, :], ALU.mult))
        p.op("vector", lambda e: e.tensor_tensor(ht[:, 1, :], ht[:, 0, :], cpt[:, 1, :], ALU.add))
        p.op("vector", lambda e: e.tensor_tensor(ht[:, 0, :], ht[:, 1, :], ft[:, :], ALU.mult))
        if wt_free[wsl]:
            p.wait("vector", pe, wt_free[wsl])
        p.op("vector", lambda e, wsl=wsl: e.tensor_tensor(wtile[:, wsl, :], ht[:, 0, :], cpt[:, 0, :], ALU.add), inc=dve)
        w_done = dve.n

        p.wait("tensor", dve, w_done)
        if tr_free[tsl]:
            p.wait("tensor", dve, tr_free[tsl])
        p.op("tensor", lambda e, wsl=wsl, tsl=tsl: e.transpose(tr_ps[:, tsl, :], wtile[:, wsl, :], T["ident_bf"][:, :]), inc=pe)
        wt_free[wsl] = pe.n
        tr_done = pe.n

        ssl = gidx % 2
        if tloc == 0 and stag_free[ssl]:
            p.wait("vector", stag_ch[ssl], stag_free[ssl])
        p.wait("vector", pe, tr_done)
        if kind != "down":
            src = tr_ps[:, tsl, :].rearrange("p (io hb) -> p hb io", hb=16)
            dst = stag[:, ssl, 0:2048].rearrange("p (hb i) -> p hb i", hb=16)[:, :, tloc * 8:tloc * 8 + 8]
        else:
            src = tr_ps[:, tsl, :].rearrange("p (q s) -> p s q", q=2)
            dst = stag[:, ssl, :].rearrange("p (s h) -> p s h", s=64)[:, :, tloc * 2:tloc * 2 + 2]
        p.op("vector", lambda e, src=src, dst=dst: e.tensor_copy(dst, src), inc=dve)
        tr_free[tsl] = dve.n
        last_stg = dve.n
        if tloc == GROUP - 1:
            p.wait("sync", dve, last_stg)
            if kind != "down":
                outap = nout_ap.rearrange("(hb c) i -> c hb i", c=128)[:, :, gidx * 128:(gidx + 1) * 128]
                srcap = stag[:, ssl, 0:2048].rearrange("p (a b) -> p a b", a=16)
            else:
                outap = nout_ap.rearrange("(s c) h -> c s h", c=128)[:, :, gidx * 128:(gidx + 1) * 128]
                srcap = stag[:, ssl, :].rearrange("p (a b) -> p a b", a=64)
            p.dma("sync", stag_ch[ssl], outap, srcap)
            stag_free[ssl] = stag_ch[ssl].n
    return last_stg



def build_L1():
    nc = bass.Bass("TRN2", target_bir_lowering=False, num_devices=NCORE)

    xT = nc.dram_tensor("xT", [D, TOK], F32, kind="ExternalInput").ap()
    wk_l = nc.dram_tensor("wk_l", [D, D], F32, kind="ExternalInput").ap()
    wq_l = nc.dram_tensor("wq_l", [D, D], F32, kind="ExternalInput").ap()
    wv_l = nc.dram_tensor("wv_l", [D, D], F32, kind="ExternalInput").ap()
    gwin_i = nc.dram_tensor("gwin", [3, 128, WTILES * 5], F32, kind="ExternalInput").ap()
    f0cb_i = nc.dram_tensor("f0cb", [3, 2, 128, WTILES], F32, kind="ExternalInput").ap()
    cdelta_i = nc.dram_tensor("cdelta", [128, 128], F32, kind="ExternalInput").ap()
    iota_i = nc.dram_tensor("iota", [128, 128], F32, kind="ExternalInput").ap()
    ident_i = nc.dram_tensor("ident", [128, 128], F32, kind="ExternalInput").ap()
    ones_i = nc.dram_tensor("ones_row", [1, 128], F32, kind="ExternalInput").ap()
    invd_i = nc.dram_tensor("invd_col_i", [128, 1], F32, kind="ExternalInput").ap()
    consts_i = nc.dram_tensor("consts_i", [128, 8], F32, kind="ExternalInput").ap()

    w_gate = nc.dram_tensor("w_gate", [D, GATE_IB], BF16, kind="ExternalOutput").ap()
    w_up = nc.dram_tensor("w_up", [D, GATE_IB], BF16, kind="ExternalOutput").ap()
    w_down = nc.dram_tensor("w_down", [INTER, DOWN_HB], BF16, kind="ExternalOutput").ap()
    k_pub = nc.dram_tensor("k_pub", [H_A, HD, TOK], BF16, kind="ExternalOutput").ap()
    v_pub = nc.dram_tensor("v_pub", [H_A, TOK, HD], BF16, kind="ExternalOutput").ap()
    q_pub = nc.dram_tensor("q_pub", [H_A, HD, TOK], BF16, kind="ExternalOutput").ap()

    with ExitStack() as ctx:
        e = ctx.enter_context
        ln_all = e(nc.sbuf_tensor("ln_all", [128, DT_PER, TOK], BF16))
        xblk = e(nc.sbuf_tensor("xblk", [128, DT_PER, TB], F32))
        xcast = e(nc.sbuf_tensor("xcast", [128, DT_PER, TB], BF16))
        xsq = e(nc.sbuf_tensor("xsq", [128, DT_PER, TB], BF16))
        rows4 = e(nc.sbuf_tensor("rows4", [1, 4, TB], F32))
        ln_t = e(nc.sbuf_tensor("ln_t", [128, TB], F32))
        invd_col = e(nc.sbuf_tensor("invd_col", [128, 1], BF16))
        const_sb = e(nc.sbuf_tensor("const_sb", [128, 8], F32))
        ones_f32 = e(nc.sbuf_tensor("ones_f32", [1, 128], F32))
        ident_bf = e(nc.sbuf_tensor("ident_bf", [128, 128], BF16))
        cdelta = e(nc.sbuf_tensor("cdelta_s", [128, 128], F32))
        iota_c = e(nc.sbuf_tensor("iota_s", [128, 128], F32))
        f0cb_s = e(nc.sbuf_tensor("f0cb_s", [128, 3, 2, WTILES], F32))
        g_all = e(nc.sbuf_tensor("g_all", [128, 3, 5 * WTILES], F32))
        coef = e(nc.sbuf_tensor("coef", [128, 8, WTILES], F32))
        cscr = e(nc.sbuf_tensor("cscr", [128, 2, WTILES], F32))
        mt = e(nc.sbuf_tensor("mt", [128, 128], F32))
        ft = e(nc.sbuf_tensor("ft", [128, 128], F32))
        cpt = e(nc.sbuf_tensor("cpt", [128, 4, 128], F32))
        ht = e(nc.sbuf_tensor("ht", [128, 2, 128], F32))
        wtile = e(nc.sbuf_tensor("wtile", [128, 2, 128], BF16))
        stag = e(nc.sbuf_tensor("stag", [128, 2, 8192], BF16))
        wslab = e(nc.sbuf_tensor("wslab", [128, 2, DT_PER, 128], BF16))
        kqv_ev = e(nc.sbuf_tensor("kqv_ev", [128, 2, TB], BF16))
        vt_ev = e(nc.sbuf_tensor("vt_ev", [128, 2, 4, 128], BF16))

        psA = e(nc.psum_tensor("psA", [128, 2 * TB], F32))
        bcA = e(nc.psum_tensor("bcA", [128, 2, TB], F32))
        proj_ps = e(nc.psum_tensor("proj_ps", [128, 2, TB], F32))
        tr_ps = e(nc.psum_tensor("tr_ps", [128, 4, 128], BF16))

        block = e(nc.Block())
        p = Prog(nc, ctx)
        pe, dve, act = p.ch("pe"), p.ch("dve"), p.ch("act")
        gc = p.ch("gconst")
        sc = p.ch("sconst")
        chx = p.ch("chx")
        slab_ch = [p.ch("slab0"), p.ch("slab1")]
        stag_ch = [p.ch("stag0"), p.ch("stag1")]
        kqv_ch = [p.ch("kqv0"), p.ch("kqv1")]
        vt_ch = [p.ch("vt0"), p.ch("vt1")]

        C = dict(xcast=xcast, xsq=xsq, psA=psA, bcA=bcA, rows4=rows4, ln_t=ln_t,
                 invd_col=invd_col[:, :], ones_f32=ones_f32[:, :],
                 eps1=const_sb[0:1, 1:2], zero1=const_sb[0:1, 0:1])
        T = dict(g_all=g_all, coef=coef, cscr=cscr, f0cb_s=f0cb_s,
                 mt=mt, ft=ft, cpt=cpt, ht=ht, wtile=wtile, stag=stag,
                 tr_ps=tr_ps, iota_c=iota_c, cdelta=cdelta, ident_bf=ident_bf)

        p.dma("gpsimd", gc, invd_col[:, :], invd_i)
        p.dma("sync", sc, const_sb[:, :], consts_i)
        p.dma("gpsimd", gc, ident_bf[:, :], ident_i)
        p.dma("sync", sc, ones_f32[:, :], ones_i)
        p.dma("sync", sc, cdelta[:, :], cdelta_i)
        p.dma("sync", sc, iota_c[:, :], iota_i)
        for w in range(3):
            p.dma("sync", sc, f0cb_s[:, w, 0, :], f0cb_i[w, 0, :, :])
            p.dma("sync", sc, f0cb_s[:, w, 1, :], f0cb_i[w, 1, :, :])
            p.dma("sync", sc, g_all[:, w, :], gwin_i[w, :, :])
        const_sc = sc.n
        const_gc = gc.n

        p.wait("vector", sc, const_sc)
        p.wait("tensor", gc, const_gc)
        p.wait("tensor", sc, const_sc)
        p.wait("scalar", sc, const_sc)

        xT_t = xT.rearrange("(dt p) t -> p dt t", p=128)
        for tb in range(NTB):
            p.dma("sync", chx, xblk[:, :, :], xT_t[:, :, tb * TB:(tb + 1) * TB])
            p.wait("vector", chx, chx.n)
            d = emit_layernorm(p, xblk, ln_all[:, :, tb * TB:(tb + 1) * TB], C)
            p.wait("sync", dve, d)
        ln_all_done = dve.n

        recon_state = {}
        for w, (out_ap, kind) in enumerate(
                ((w_gate, "gate"), (w_up, "up"), (w_down, "down"))):
            recon_stg = emit_recon_weight(p, T, w, out_ap, kind, stag_ch, recon_state)

        slab_free = [0, 0]
        ev_free = [0, 0]
        kqv_free_ch = [None, None]
        vt_free = [0, 0]
        proj_free = [0, 0]
        tr_ps_guard = recon_stg
        wsrc = {0: wk_l, 1: wq_l, 2: wv_l}
        wdst = {0: k_pub, 1: q_pub, 2: v_pub}
        for wt in range(3):
            wsrc_t = wsrc[wt].rearrange("(dt p) m -> p dt m", p=128)
            for h in range(H_A):
                sl = (wt * H_A + h) % 2
                if slab_free[sl]:
                    p.wait("gpsimd", pe, slab_free[sl])
                p.dma("gpsimd", slab_ch[sl], wslab[:, sl, :, :], wsrc_t[:, :, h * 128:(h + 1) * 128])
                p.wait("tensor", slab_ch[sl], slab_ch[sl].n)
                p.wait("tensor", dve, ln_all_done)
                for tb in range(NTB):
                    psl = tb % 2
                    if proj_free[psl]:
                        p.wait("tensor", act, proj_free[psl])
                    for dt in range(DT_PER):
                        p.op("tensor", lambda e, sl=sl, psl=psl, dt=dt, tb=tb: e.matmul(
                            proj_ps[:, psl, :], wslab[:, sl, dt, :],
                            ln_all[:, dt, tb * TB:(tb + 1) * TB],
                            start=(dt == 0), stop=(dt == DT_PER - 1)),
                            inc=(pe if dt == DT_PER - 1 else None))
                    mm_done = pe.n
                    esl = tb % 2
                    if ev_free[esl]:
                        p.wait("scalar", kqv_free_ch[esl], ev_free[esl])
                    p.wait("scalar", pe, mm_done)
                    p.op("scalar", lambda e, esl=esl, psl=psl: e.copy(kqv_ev[:, esl, :], proj_ps[:, psl, :]), inc=act)
                    ev_done = act.n
                    proj_free[psl] = ev_done
                    if wt != 2:
                        p.wait("sync", act, ev_done)
                        p.dma("sync", kqv_ch[esl], wdst[wt][h, :, tb * TB:(tb + 1) * TB], kqv_ev[:, esl, :])
                        ev_free[esl] = kqv_ch[esl].n
                        kqv_free_ch[esl] = kqv_ch[esl]
                    else:
                        p.wait("tensor", act, ev_done)
                        if tr_ps_guard:
                            p.wait("tensor", dve, tr_ps_guard)
                            tr_ps_guard = 0
                        if vt_free[esl]:
                            p.wait("tensor", act, vt_free[esl])
                        for qq in range(4):
                            p.op("tensor", lambda e, esl=esl, qq=qq: e.transpose(
                                tr_ps[:, qq, :], kqv_ev[:, esl, qq * 128:(qq + 1) * 128],
                                ident_bf[:, :]), inc=(pe if qq == 3 else None))
                        trs_done = pe.n
                        p.wait("scalar", pe, trs_done)
                        p.op("scalar", lambda e, esl=esl: e.copy(
                            vt_ev[:, esl, :, :], tr_ps[:, :, :]), inc=act)
                        vt_done = act.n
                        vt_free[esl] = vt_done
                        p.wait("sync", act, vt_done)
                        p.dma("sync", vt_ch[esl],
                              v_pub[h, tb * TB:(tb + 1) * TB, :].rearrange("(q t) d -> t q d", t=128),
                              vt_ev[:, esl, :, :])
                        ev_free[esl] = vt_ch[esl].n
                        kqv_free_ch[esl] = vt_ch[esl]
                slab_free[sl] = pe.n

        for chn in (gc, sc, chx, *slab_ch, *stag_ch, *kqv_ch, *vt_ch):
            if chn.n:
                p.wait("gpsimd", chn, chn.n)

        p.finalize(block)
    return nc



_build_cache = {}


def _f32(x):
    return np.ascontiguousarray(np.asarray(x), dtype=np.float32)


def run_L1(inputs):
    x0 = _f32(inputs["L0_attn_a_x"])
    W_K = _f32(inputs["L0_attn_a_W_K"])
    W_Q = _f32(inputs["L0_attn_a_W_Q"])
    W_V = _f32(inputs["L0_attn_a_W_V"])
    cps = [_f32(inputs["L0_ffn_gate_cp"]), _f32(inputs["L0_ffn_up_cp"]),
           _f32(inputs["L0_ffn_down_cp"])]
    wk_l = np.ascontiguousarray(W_K.transpose(2, 0, 1).reshape(D, D))
    wq_l = np.ascontiguousarray(W_Q.transpose(2, 0, 1).reshape(D, D))
    wv_l = np.ascontiguousarray(W_V.transpose(2, 0, 1).reshape(D, D))

    if "L1" not in _build_cache:
        _build_cache["L1"] = build_L1()
    nc1 = _build_cache["L1"]

    in_maps = []
    for c in range(NCORE):
        hc = host_constants(c)
        pos = core_positions(c)
        b = c // 4
        in_maps.append(dict(
            xT=np.ascontiguousarray(x0[b].T[:, pos]),
            wk_l=wk_l, wq_l=wq_l, wv_l=wv_l,
            gwin=host_gwin(c, cps),
            f0cb=hc["f0cb"], cdelta=hc["cdelta"], iota=hc["iota"],
            ident=hc["ident"], ones_row=hc["ones_row"], invd_col_i=hc["invd_col"],
            consts_i=host_consts_vals(),
        ))
    return run_bass_kernel_spmd(nc1, in_maps, core_ids=list(range(NCORE)))


def kernel(**inputs):
    res1 = run_L1(inputs)
    res2 = run_L2(inputs, res1)
    res3 = run_L3(inputs, res2)
    out = np.empty((B, S, D), np.float32)
    for c in range(NCORE):
        b = c // 4
        pos = core_positions(c)
        out[b, pos, :] = res3.results[c]["outT"].T
    return out



def attn_unit_table(core):
    cA, cB = core_chunks(core)
    wA = np.zeros(9, np.float32)
    wB = np.zeros(9, np.float32)
    wA[0] = 1.0
    wB[1] = 1.0
    sel = [("A", j) for j in range(cA)] + [("B", j) for j in range(cB)]
    assert len(sel) == 7
    for u, (qs, j) in enumerate(sel, start=2):
        (wA if qs == "A" else wB)[u] = 1.0
    return wA, wB, sel


def host_kv_slots(core, pubK, pubV, nheads, hd=HD):
    _, _, sel = attn_unit_table(core)
    b = core // 4
    kk = np.empty((nheads, 7, hd, CHUNK), dtype=pubK[0].dtype)
    vv = np.empty((nheads, 7, CHUNK, hd), dtype=pubV[0].dtype)
    for u, (qs, j) in enumerate(sel):
        og = j if j < 4 else 7 - j
        owner = b * 4 + og
        span = slice(0, CHUNK) if j < 4 else slice(CHUNK, TOK)
        kk[:, u] = pubK[owner][:, :, span]
        vv[:, u] = pubV[owner][:, span, :]
    return kk, vv


def emit_attn_core(p, A, nheads, exp_scale):
    pe, dve, act = p.ch("pe"), p.ch("dve"), p.ch("act")
    qa_sb, kk_sb, vv_sb = A["qa_sb"], A["kk_sb"], A["vv_sb"]
    qu_sb, pt_sb = A["qu_sb"], A["pt_sb"]
    yA, yB, dA, dB = A["yA"], A["yB"], A["dA"], A["dB"]
    st_ps, y_ps, den_ps, rc_ps = A["st_ps"], A["y_ps"], A["den_ps"], A["rc_ps"]
    masks = A["masks"]
    z_buf = A["z_buf"]

    q_free = [0, 0]
    kv_free = [0, 0]
    st_free = [0, 0]
    pt_free = [0, 0]
    y_free = 0
    rc_free = 0

    A["load_q"](0, 0)
    A["load_kv"](0, 0, 0)
    for h in range(nheads):
        qsl = h % 2
        if h + 1 < nheads:
            nqsl = (h + 1) % 2
            if q_free[nqsl]:
                p.wait(A["q_eng"], pe, q_free[nqsl])
            A["load_q"](h + 1, nqsl)
        p.wait("tensor", A["q_ch"][qsl], A["q_n"][qsl])
        p.wait("vector", A["q_ch"][qsl], A["q_n"][qsl])
        for u in range(9):
            ksl = (h * 9 + u) % 2
            nh, nu = (h, u + 1) if u < 8 else (h + 1, 0)
            if nh < nheads:
                nksl = (h * 9 + u + 1) % 2
                if kv_free[nksl]:
                    p.wait(A["kv_eng"], pe, kv_free[nksl])
                A["load_kv"](nh, nu, nksl)
            p.wait("tensor", A["kv_chk"][ksl], A["kv_nk"][ksl])
            p.wait("tensor", A["kv_chv"][ksl], A["kv_nv"][ksl])
            if u == 0:
                qrhs = qa_sb[:, qsl, 0:CHUNK]
            elif u == 1:
                qrhs = qa_sb[:, qsl, CHUNK:TOK]
            else:
                p.op("vector", lambda e, qsl=qsl, u=u: e.tensor_scalar(
                    qu_sb[:, :], qa_sb[:, qsl, 0:CHUNK], A["uwA"][:, u:u + 1], None, ALU.mult))
                p.op("vector", lambda e, qsl=qsl, u=u: e.scalar_tensor_tensor(
                    qu_sb[:, :], qa_sb[:, qsl, CHUNK:TOK], A["uwB"][:, u:u + 1],
                    qu_sb[:, :], ALU.mult, ALU.add), inc=dve)
                p.wait("tensor", dve, dve.n)
                qrhs = qu_sb[:, :]
            diag = u < 2
            bias_col = A["bias_col"](h)
            for s in range(8):
                ktile = kk_sb[:, ksl, s * 128:(s + 1) * 128]
                vtile = vv_sb[:, ksl, s, :]
                psl = s % 2
                for h2 in range(2):
                    if st_free[h2]:
                        p.wait("tensor", act, st_free[h2])
                    p.op("tensor", lambda e, ktile=ktile, qrhs=qrhs, h2=h2: e.matmul(
                        st_ps[:, h2, :], ktile, qrhs[:, h2 * 512:(h2 + 1) * 512],
                        start=True, stop=True), inc=pe)
                    st_mm = pe.n
                    p.wait("scalar", pe, st_mm)
                    if pt_free[psl]:
                        p.wait("scalar", pe, pt_free[psl])
                    p.op("scalar", lambda e, h2=h2, psl=psl, bias_col=bias_col: e.activation(
                        pt_sb[:, psl, h2 * 512:(h2 + 1) * 512],
                        st_ps[:, h2, :], AF.Exp,
                        bias=bias_col, scale=exp_scale), inc=act)
                    st_free[h2] = act.n
                exp_done = act.n
                if diag:
                    p.wait("vector", act, exp_done)
                    if pt_free[psl]:
                        p.wait("vector", pe, pt_free[psl])
                    p.op("vector", lambda e, s=s, psl=psl: e.tensor_tensor(
                        pt_sb[:, psl, 0:512], pt_sb[:, psl, 0:512],
                        masks[:, s, 0, :], ALU.mult))
                    p.op("vector", lambda e, s=s, psl=psl: e.tensor_tensor(
                        pt_sb[:, psl, 512:1024], pt_sb[:, psl, 512:1024],
                        masks[:, s, 1, :], ALU.mult), inc=dve)
                    p.wait("tensor", dve, dve.n)
                else:
                    p.wait("tensor", act, exp_done)
                if s == 0 and y_free:
                    p.wait("tensor", dve, y_free)
                for h2 in range(2):
                    p.op("tensor", lambda e, vtile=vtile, h2=h2, s=s, psl=psl: e.matmul(
                        y_ps[:, h2, :], vtile, pt_sb[:, psl, h2 * 512:(h2 + 1) * 512],
                        start=(s == 0), stop=(s == 7)),
                        inc=(pe if (s == 7 and h2 == 1) else None))
                    p.op("tensor", lambda e, h2=h2, s=s, psl=psl: e.matmul(
                        den_ps[0:1, h2, :], A["onesb"], pt_sb[:, psl, h2 * 512:(h2 + 1) * 512],
                        start=(s == 0), stop=(s == 7)),
                        inc=(pe if (s == 7 and h2 == 1) else None))
                pt_free[psl] = pe.n
            pv_done = pe.n
            kv_free[ksl] = pv_done
            p.wait("vector", pe, pv_done)
            ypsf = y_ps[:, :, :].rearrange("p a b -> p (a b)")
            dpsf = den_ps[0:1, :, :].rearrange("p a b -> p (a b)")
            if u == 0:
                p.op("vector", lambda e: e.tensor_scalar(yA[:, :], ypsf, A["uwA"][:, 0:1], None, ALU.mult))
                p.op("vector", lambda e: e.tensor_scalar(dA[:, :], dpsf, A["uwA"][0:1, 0:1], None, ALU.mult), inc=dve)
            elif u == 1:
                p.op("vector", lambda e: e.tensor_scalar(yB[:, :], ypsf, A["uwB"][:, 1:2], None, ALU.mult))
                p.op("vector", lambda e: e.tensor_scalar(dB[:, :], dpsf, A["uwB"][0:1, 1:2], None, ALU.mult), inc=dve)
            else:
                p.op("vector", lambda e, u=u: e.scalar_tensor_tensor(
                    yA[:, :], ypsf, A["uwA"][:, u:u + 1], yA[:, :], ALU.mult, ALU.add))
                p.op("vector", lambda e, u=u: e.scalar_tensor_tensor(
                    dA[:, :], dpsf, A["uwA"][0:1, u:u + 1], dA[:, :], ALU.mult, ALU.add))
                p.op("vector", lambda e, u=u: e.scalar_tensor_tensor(
                    yB[:, :], ypsf, A["uwB"][:, u:u + 1], yB[:, :], ALU.mult, ALU.add))
                p.op("vector", lambda e, u=u: e.scalar_tensor_tensor(
                    dB[:, :], dpsf, A["uwB"][0:1, u:u + 1], dB[:, :], ALU.mult, ALU.add), inc=dve)
            y_free = dve.n
        q_free[qsl] = pe.n
        p.op("vector", lambda e: e.reciprocal(A["rrow"][:, 0, :], dA[:, :]))
        p.op("vector", lambda e: e.reciprocal(A["rrow"][:, 1, :], dB[:, :]), inc=dve)
        p.wait("tensor", dve, dve.n)
        if rc_free:
            p.wait("tensor", dve, rc_free)
        zi = A["zmap"](h)
        for part, y_acc in ((0, yA), (1, yB)):
            for h2 in range(2):
                p.op("tensor", lambda e, part=part, h2=h2: e.matmul(
                    rc_ps[:, h2, :], A["ones_f32"],
                    A["rrow"][:, part, h2 * 512:(h2 + 1) * 512],
                    start=True, stop=True), inc=(pe if h2 == 1 else None))
            rc_mm = pe.n
            p.wait("vector", pe, rc_mm)
            p.op("vector", lambda e, zi=zi, part=part, y_acc=y_acc: e.tensor_tensor(
                z_buf[:, zi, part * CHUNK:(part + 1) * CHUNK], y_acc[:, :],
                rc_ps[:, :, :].rearrange("p a b -> p (a b)"), ALU.mult), inc=dve)
            rc_free = dve.n
            if part == 0:
                p.wait("tensor", dve, rc_free)
        y_free = max(y_free, rc_free)
    return dve.n



def emit_ffn(p, Fd, x_src_ap, x_dst_ap, C):
    pe, dve, act = p.ch("pe"), p.ch("dve"), p.ch("act")
    gslab, dslab = Fd["gslab"], Fd["dslab"]
    at_sb = Fd["at_sb"]
    xin_sb, xout_sb = Fd["xin_sb"], Fd["xout_sb"]
    gu_ps, dn_ps = Fd["gu_ps"], Fd["dn_ps"]
    sil_sb = Fd["sil_sb"]
    ln_out = C["xcast"]

    gfree = [0, 0]
    dfree = [0, 0]
    gu_free = [0, 0]
    dn_free = [0, 0]
    xi_free = 0
    xo_free = 0
    at_free = 0

    wg_t = Fd["wg"].rearrange("(dt p) i -> p dt i", p=128)
    wu_t = Fd["wu"].rearrange("(dt p) i -> p dt i", p=128)
    wd_t = Fd["wd"].rearrange("(it p) h -> p it h", p=128)

    for tb in range(NTB):
        if xi_free:
            p.wait(Fd["xi_eng"], dve, xi_free)
        if xo_free:
            p.wait(Fd["xi_eng"], Fd["ch_xo"], xo_free)
        p.dma(Fd["xi_eng"], Fd["ch_xi"], xin_sb[:, :, :], x_src_ap[:, tb * TB:(tb + 1) * TB].rearrange("(dt p) t -> p dt t", p=128))
        p.wait("vector", Fd["ch_xi"], Fd["ch_xi"].n)
        ln_done = emit_layernorm(p, xin_sb, ln_out, C)

        if at_free:
            p.wait("vector", pe, at_free)
        for it in range(IT_PER):
            gsl = it % 2
            if gfree[gsl]:
                p.wait(Fd["g_eng"], pe, gfree[gsl])
            p.dma(Fd["g_eng"], Fd["ch_g"][gsl], gslab[:, gsl, 0, :, :], wg_t[:, :, it * 128:(it + 1) * 128])
            p.dma(Fd["g_eng"], Fd["ch_g"][gsl], gslab[:, gsl, 1, :, :], wu_t[:, :, it * 128:(it + 1) * 128])
            gn = Fd["ch_g"][gsl].n
            p.wait("tensor", Fd["ch_g"][gsl], gn)
            p.wait("tensor", dve, ln_done)
            for j in range(2):
                if gu_free[j]:
                    p.wait("tensor", act if j == 0 else dve, gu_free[j])
                for dt in range(DT_PER):
                    p.op("tensor", lambda e, gsl=gsl, j=j, dt=dt: e.matmul(
                        gu_ps[:, j, :], gslab[:, gsl, j, dt, :], ln_out[:, dt, :],
                        start=(dt == 0), stop=(dt == DT_PER - 1)),
                        inc=(pe if dt == DT_PER - 1 else None))
            mm_done = pe.n
            gfree[gsl] = mm_done
            p.wait("scalar", pe, mm_done)
            p.op("scalar", lambda e: e.activation(sil_sb[:, :], gu_ps[:, 0, :], AF.Silu,
                                                  bias=C["zero128"], scale=1.0), inc=act)
            gu_free[0] = act.n
            p.wait("vector", act, act.n)
            p.op("vector", lambda e, it=it: e.tensor_tensor(
                at_sb[:, it, :], sil_sb[:, :], gu_ps[:, 1, :], ALU.mult), inc=dve)
            gu_free[1] = dve.n
        a_done = dve.n

        p.wait("tensor", dve, a_done)
        if xo_free:
            p.wait("vector", Fd["ch_xo"], xo_free)
        for dt in range(DT_PER):
            dsl = dt % 2
            if dfree[dsl]:
                p.wait(Fd["d_eng"], pe, dfree[dsl])
            p.dma(Fd["d_eng"], Fd["ch_d"][dsl], dslab[:, dsl, :, :], wd_t[:, :, dt * 128:(dt + 1) * 128])
            dn_ = Fd["ch_d"][dsl].n
            p.wait("tensor", Fd["ch_d"][dsl], dn_)
            psl = dt % 2
            if dn_free[psl]:
                p.wait("tensor", dve, dn_free[psl])
            for it in range(IT_PER):
                p.op("tensor", lambda e, dsl=dsl, psl=psl, it=it: e.matmul(
                    dn_ps[:, psl, :], dslab[:, dsl, it, :], at_sb[:, it, :],
                    start=(it == 0), stop=(it == IT_PER - 1)),
                    inc=(pe if it == IT_PER - 1 else None))
            dmm = pe.n
            dfree[dsl] = dmm
            p.wait("vector", pe, dmm)
            p.op("vector", lambda e, dt=dt, psl=psl: e.tensor_tensor(
                xout_sb[:, dt, :], xin_sb[:, dt, :], dn_ps[:, psl, :], ALU.add), inc=dve)
            dn_free[psl] = dve.n
        at_free = pe.n
        xi_free = dve.n
        add_done = dve.n
        p.wait(Fd["xo_eng"], dve, add_done)
        p.dma(Fd["xo_eng"], Fd["ch_xo"], x_dst_ap[:, tb * TB:(tb + 1) * TB].rearrange("(dt p) t -> p dt t", p=128), xout_sb[:, :, :])
        xo_free = Fd["ch_xo"].n
    return Fd["ch_xo"].n


def _barrier(p):
    for eng in ENGINES:
        for ch in list(p.chs.values()):
            if ch.n:
                p.wait(eng, ch, ch.n)


def build_L2():
    nc = bass.Bass("TRN2", target_bir_lowering=False, num_devices=NCORE)

    xT = nc.dram_tensor("xT", [D, TOK], F32, kind="ExternalInput").ap()
    qa_i = nc.dram_tensor("qa_i", [H_A, HD, TOK], BF16, kind="ExternalInput").ap()
    ka_own = nc.dram_tensor("ka_own", [H_A, HD, TOK], BF16, kind="ExternalInput").ap()
    va_own = nc.dram_tensor("va_own", [H_A, TOK, HD], BF16, kind="ExternalInput").ap()
    ka_sl = nc.dram_tensor("ka_sl", [H_A, 7, HD, CHUNK], BF16, kind="ExternalInput").ap()
    va_sl = nc.dram_tensor("va_sl", [H_A, 7, CHUNK, HD], BF16, kind="ExternalInput").ap()
    uw_i = nc.dram_tensor("uw_i", [2, 128, 9], F32, kind="ExternalInput").ap()
    masks_i = nc.dram_tensor("masks_i", [128, 8, 2, 512], F32, kind="ExternalInput").ap()
    wo_l = nc.dram_tensor("wo_l", [D, D], F32, kind="ExternalInput").ap()
    wg_i = nc.dram_tensor("wg_i", [D, INTER], BF16, kind="ExternalInput").ap()
    wu_i = nc.dram_tensor("wu_i", [D, INTER], BF16, kind="ExternalInput").ap()
    wd_i = nc.dram_tensor("wd_i", [INTER, D], BF16, kind="ExternalInput").ap()
    wqb_i = nc.dram_tensor("wqb_i", [D, D], F32, kind="ExternalInput").ap()
    wkb_i = nc.dram_tensor("wkb_i", [D, KVD], F32, kind="ExternalInput").ap()
    wvb_i = nc.dram_tensor("wvb_i", [D, KVD], F32, kind="ExternalInput").ap()
    qA_i = nc.dram_tensor("qA_i", [D, RANK], F32, kind="ExternalInput").ap()
    qB_i = nc.dram_tensor("qB_i", [RANK, D], F32, kind="ExternalInput").ap()
    kA_i = nc.dram_tensor("kA_i", [D, RANK], F32, kind="ExternalInput").ap()
    kB_i = nc.dram_tensor("kB_i", [RANK, KVD], F32, kind="ExternalInput").ap()
    vA_i = nc.dram_tensor("vA_i", [D, RANK], F32, kind="ExternalInput").ap()
    vB_i = nc.dram_tensor("vB_i", [RANK, KVD], F32, kind="ExternalInput").ap()
    qgain_i = nc.dram_tensor("qgain_i", [1, H_A], F32, kind="ExternalInput").ap()
    cos_i = nc.dram_tensor("cos_i", [128, TOK], F32, kind="ExternalInput").ap()
    sin_i = nc.dram_tensor("sin_i", [128, TOK], F32, kind="ExternalInput").ap()
    jt_i = nc.dram_tensor("jt_i", [128, 128], F32, kind="ExternalInput").ap()
    ident_i = nc.dram_tensor("ident", [128, 128], F32, kind="ExternalInput").ap()
    ones_i = nc.dram_tensor("ones_row", [1, 128], F32, kind="ExternalInput").ap()
    invd_i = nc.dram_tensor("invd_col_i", [128, 1], F32, kind="ExternalInput").ap()
    invh_i = nc.dram_tensor("invh_col_i", [128, 1], F32, kind="ExternalInput").ap()
    onesc_i = nc.dram_tensor("onesc_i", [128, 1], F32, kind="ExternalInput").ap()
    consts_i = nc.dram_tensor("consts_i", [128, 8], F32, kind="ExternalInput").ap()

    x2T = nc.dram_tensor("x2T", [D, TOK], F32, kind="ExternalOutput").ap()
    kb_pub = nc.dram_tensor("kb_pub", [HKV, HD, TOK], BF16, kind="ExternalOutput").ap()
    vb_pub = nc.dram_tensor("vb_pub", [HKV, TOK, HD], BF16, kind="ExternalOutput").ap()
    vbT_own = nc.dram_tensor("vbT_own", [HKV, HD, TOK], BF16, kind="ExternalOutput").ap()
    qb_pub = nc.dram_tensor("qb_pub", [H_A, HD, TOK], BF16, kind="ExternalOutput").ap()
    resid_d = nc.dram_tensor("resid_d", [D, TOK], F32)

    with ExitStack() as ctx:
        e = ctx.enter_context
        xcast = e(nc.sbuf_tensor("xcast", [128, DT_PER, TB], BF16))
        xsq = e(nc.sbuf_tensor("xsq", [128, DT_PER, TB], BF16))
        rows4 = e(nc.sbuf_tensor("rows4", [1, 4, TB], F32))
        ln_t = e(nc.sbuf_tensor("ln_t", [128, TB], F32))
        invd_col = e(nc.sbuf_tensor("invd_col", [128, 1], BF16))
        invh_col = e(nc.sbuf_tensor("invh_col", [128, 1], BF16))
        onesb_col = e(nc.sbuf_tensor("onesb_col", [128, 1], BF16))
        ones_f32 = e(nc.sbuf_tensor("ones_f32", [1, 128], F32))
        const_sb = e(nc.sbuf_tensor("const_sb", [128, 8], F32))
        uw_sb = e(nc.sbuf_tensor("uw_sb", [128, 2, 9], F32))
        masks_sb = e(nc.sbuf_tensor("masks_sb", [128, 8, 2, 512], BF16))
        qg_sb = e(nc.sbuf_tensor("qg_sb", [1, H_A], F32))

        block = e(nc.Block())
        p = Prog(nc, ctx)
        pe, dve, act = p.ch("pe"), p.ch("dve"), p.ch("act")
        gc, sc = p.ch("gconst"), p.ch("sconst")

        C = dict(xcast=xcast, xsq=xsq, rows4=rows4, ln_t=ln_t,
                 invd_col=invd_col[:, :], ones_f32=ones_f32[:, :],
                 eps1=const_sb[0:1, 1:2], zero1=const_sb[0:1, 0:1],
                 zero128=const_sb[:, 0:1],
                 epsr1=const_sb[0:1, 2:3], shiftA=const_sb[:, 3:4])

        p.dma("gpsimd", gc, invd_col[:, :], invd_i)
        p.dma("gpsimd", gc, invh_col[:, :], invh_i)
        p.dma("gpsimd", gc, onesb_col[:, :], onesc_i)
        p.dma("gpsimd", gc, masks_sb[:, :, :, :], masks_i)
        p.dma("sync", sc, ones_f32[:, :], ones_i)
        p.dma("sync", sc, const_sb[:, :], consts_i)
        p.dma("sync", sc, uw_sb[:, 0, :], uw_i[0, :, :])
        p.dma("sync", sc, uw_sb[:, 1, :], uw_i[1, :, :])
        p.dma("sync", sc, qg_sb[:, :], qgain_i)
        for eng in ("tensor", "vector", "scalar"):
            p.wait(eng, gc, gc.n)
            p.wait(eng, sc, sc.n)

        with ExitStack() as s1:
            e1 = s1.enter_context
            z_buf = e1(nc.sbuf_tensor("z_buf", [128, H_A, TOK], BF16))
            qa_sb = e1(nc.sbuf_tensor("qa_sb", [128, 2, TOK], BF16))
            kk_sb = e1(nc.sbuf_tensor("kk_sb", [128, 2, CHUNK], BF16))
            vv_sb = e1(nc.sbuf_tensor("vv_sb", [128, 2, 8, 128], BF16))
            qu_sb = e1(nc.sbuf_tensor("qu_sb", [128, CHUNK], BF16))
            pt_sb = e1(nc.sbuf_tensor("pt_sb", [128, 2, CHUNK], BF16))
            yA = e1(nc.sbuf_tensor("yA", [128, CHUNK], F32))
            yB = e1(nc.sbuf_tensor("yB", [128, CHUNK], F32))
            dAr = e1(nc.sbuf_tensor("dAr", [1, CHUNK], F32))
            dBr = e1(nc.sbuf_tensor("dBr", [1, CHUNK], F32))
            rrow = e1(nc.sbuf_tensor("rrow", [1, 2, CHUNK], F32))
            st_ps = e1(nc.psum_tensor("st_ps", [128, 2, 512], F32))
            y_ps = e1(nc.psum_tensor("y_ps", [128, 2, 512], F32))
            dr_ps = e1(nc.psum_tensor("dr_ps", [128, 2, 512], F32))

            qch = [p.ch("qa0"), p.ch("qa1")]
            kch = [p.ch("ka0"), p.ch("ka1")]
            vch = [p.ch("va0"), p.ch("va1")]
            A = dict(qa_sb=qa_sb, kk_sb=kk_sb, vv_sb=vv_sb, qu_sb=qu_sb,
                     pt_sb=pt_sb, yA=yA, yB=yB, dA=dAr, dB=dBr, rrow=rrow,
                     st_ps=st_ps, y_ps=y_ps, den_ps=dr_ps[0:1, :, :], rc_ps=dr_ps,
                     masks=masks_sb, z_buf=z_buf,
                     uwA=uw_sb[:, 0, :], uwB=uw_sb[:, 1, :],
                     onesb=onesb_col[:, :], ones_f32=ones_f32[:, :],
                     q_ch=qch, q_n=[0, 0], kv_chk=kch, kv_chv=vch,
                     kv_nk=[0, 0], kv_nv=[0, 0],
                     q_eng="sync", kv_eng="gpsimd",
                     zmap=lambda h: h,
                     bias_col=lambda h: C["shiftA"])

            def load_q(h, slot):
                p.dma("sync", qch[slot], qa_sb[:, slot, :], qa_i[h, :, :])
                A["q_n"][slot] = qch[slot].n
            A["load_q"] = load_q

            def load_kv(h, u, slot):
                if u == 0:
                    ksrc, vsrc = ka_own[h, :, 0:CHUNK], va_own[h, 0:CHUNK, :]
                elif u == 1:
                    ksrc, vsrc = ka_own[h, :, CHUNK:TOK], va_own[h, CHUNK:TOK, :]
                else:
                    ksrc, vsrc = ka_sl[h, u - 2, :, :], va_sl[h, u - 2, :, :]
                p.dma("gpsimd", kch[slot], kk_sb[:, slot, :], ksrc)
                A["kv_nk"][slot] = kch[slot].n
                p.dma("gpsimd", vch[slot], vv_sb[:, slot, :, :],
                      vsrc.rearrange("(c p) d -> p c d", p=128))
                A["kv_nv"][slot] = vch[slot].n
            A["load_kv"] = load_kv

            z_done = emit_attn_core(p, A, H_A, 0.08838834764831845)

            woslab = e1(nc.sbuf_tensor("woslab", [128, 2, DT_PER, 128], BF16))
            xin1 = e1(nc.sbuf_tensor("xin1", [128, TB], F32))
            wo_ps = e1(nc.psum_tensor("wo_ps", [128, 2, TB], F32))
            wo_t = wo_l.rearrange("(ft p) m -> p ft m", p=128)
            xT_t = xT.rearrange("(dt p) t -> p dt t", p=128)
            rd_t = resid_d.rearrange("(dt p) t -> p dt t", p=128)
            wo_ch = [p.ch("wo0"), p.ch("wo1")]
            x1_ch = p.ch("x1c")
            xi_ch = p.ch("xic")
            wo_free = [0, 0]
            ps_free = [0, 0]
            p.wait("tensor", dve, z_done)
            for dt in range(DT_PER):
                wsl = dt % 2
                if wo_free[wsl]:
                    p.wait("gpsimd", pe, wo_free[wsl])
                p.dma("gpsimd", wo_ch[wsl], woslab[:, wsl, :, :], wo_t[:, :, dt * 128:(dt + 1) * 128])
                p.wait("tensor", wo_ch[wsl], wo_ch[wsl].n)
                for tb in range(NTB):
                    psl = tb % 2
                    if ps_free[psl]:
                        p.wait("tensor", dve, ps_free[psl])
                    for ft in range(DT_PER):
                        p.op("tensor", lambda e, wsl=wsl, psl=psl, ft=ft, tb=tb: e.matmul(
                            wo_ps[:, psl, :], woslab[:, wsl, ft, :],
                            z_buf[:, ft, tb * TB:(tb + 1) * TB],
                            start=(ft == 0), stop=(ft == DT_PER - 1)),
                            inc=(pe if ft == DT_PER - 1 else None))
                    wo_mm = pe.n
                    p.wait("sync", dve, dve.n)
                    p.wait("sync", x1_ch, x1_ch.n)
                    p.dma("sync", xi_ch, xin1[:, :], xT_t[:, dt, tb * TB:(tb + 1) * TB])
                    p.wait("vector", xi_ch, xi_ch.n)
                    p.wait("vector", pe, wo_mm)
                    p.op("vector", lambda e, psl=psl: e.tensor_tensor(
                        xin1[:, :], xin1[:, :], wo_ps[:, psl, :], ALU.add), inc=dve)
                    ps_free[psl] = dve.n
                    p.wait("sync", dve, dve.n)
                    p.dma("sync", x1_ch, rd_t[:, dt, tb * TB:(tb + 1) * TB], xin1[:, :])
                wo_free[wsl] = pe.n

        _barrier(p)

        with ExitStack() as s2:
            e2 = s2.enter_context
            at_sb = e2(nc.sbuf_tensor("at_sb", [128, IT_PER, TB], BF16))
            gslab = e2(nc.sbuf_tensor("gslab", [128, 2, 2, DT_PER, 128], BF16))
            dslab = e2(nc.sbuf_tensor("dslab", [128, 2, IT_PER, 128], BF16))
            xin_sb = e2(nc.sbuf_tensor("xin_sb", [128, DT_PER, TB], F32))
            sil_sb = e2(nc.sbuf_tensor("sil_sb", [128, TB], BF16))
            gu_ps = e2(nc.psum_tensor("gu_ps", [128, 2, TB], F32))
            dn_ps = e2(nc.psum_tensor("dn_ps", [128, 2, TB], F32))
            psA2 = e2(nc.psum_tensor("psA2", [128, 2 * TB], F32))
            bcA2 = e2(nc.psum_tensor("bcA2", [128, 2, TB], F32))
            C["psA"] = psA2
            C["bcA"] = bcA2
            Fd = dict(wg=wg_i, wu=wu_i, wd=wd_i,
                      gslab=gslab, dslab=dslab, at_sb=at_sb,
                      xin_sb=xin_sb, xout_sb=xin_sb,
                      gu_ps=gu_ps, dn_ps=dn_ps, sil_sb=sil_sb,
                      ch_g=[p.ch("ffg0"), p.ch("ffg1")],
                      ch_d=[p.ch("ffd0"), p.ch("ffd1")],
                      ch_xi=p.ch("ffxi"), ch_xo=p.ch("ffxo"),
                      g_eng="gpsimd", d_eng="gpsimd", xi_eng="sync", xo_eng="sync")
            emit_ffn(p, Fd, resid_d, x2T, C)

        _barrier(p)

        with ExitStack() as s3:
            e3 = s3.enter_context
            cos_sb = e3(nc.sbuf_tensor("cos_sb", [128, TOK], F32))
            sin_sb = e3(nc.sbuf_tensor("sin_sb", [128, TOK], F32))
            jt_sb = e3(nc.sbuf_tensor("jt_sb", [128, 128], BF16))
            ident_bf = e3(nc.sbuf_tensor("ident_bf", [128, 128], BF16))
            bslab = e3(nc.sbuf_tensor("bslab", [128, 2, DT_PER, 128], BF16))
            adA_sb = e3(nc.sbuf_tensor("adA_sb", [128, 3, DT_PER, RANK], BF16))
            adB_sb = e3(nc.sbuf_tensor("adB_sb", [16, 3, D], BF16))
            uad_sb = e3(nc.sbuf_tensor("uad_sb", [16, 3, TB], BF16))
            xb_sb = e3(nc.sbuf_tensor("xb_sb", [128, DT_PER, TB], F32))
            qraw_sb = e3(nc.sbuf_tensor("qraw_sb", [128, TB], F32))
            sqq_sb = e3(nc.sbuf_tensor("sqq_sb", [128, TB], BF16))
            qn_sb = e3(nc.sbuf_tensor("qn_sb", [128, TB], BF16))
            r1_sb = e3(nc.sbuf_tensor("r1_sb", [128, TB], F32))
            rot_sb = e3(nc.sbuf_tensor("rot_sb", [128, 2, TB], BF16))
            vtb_sb = e3(nc.sbuf_tensor("vtb_sb", [128, 4, 128], BF16))
            mrow = e3(nc.sbuf_tensor("mrow", [1, TB], F32))
            bj_ps = e3(nc.psum_tensor("bj_ps", [128, 2, TB], F32))
            ut_ps = e3(nc.psum_tensor("ut_ps", [128, 2, TB], F32))
            tr_ps = e3(nc.psum_tensor("tr_ps", [128, 4, 128], BF16))
            C["psA"] = ut_ps[:, :, :].rearrange("p a b -> p (a b)")
            C["bcA"] = bj_ps

            bsc = p.ch("bsc")
            bscg = p.ch("bscg")
            p.dma("sync", bsc, cos_sb[:, :], cos_i)
            p.dma("sync", bsc, sin_sb[:, :], sin_i)
            p.dma("gpsimd", bscg, jt_sb[:, :], jt_i)
            p.dma("gpsimd", bscg, ident_bf[:, :], ident_i)
            p.dma("gpsimd", bscg, adA_sb[:, 0, :, :], qA_i.rearrange("(dt p) r -> p dt r", p=128))
            p.dma("gpsimd", bscg, adA_sb[:, 1, :, :], kA_i.rearrange("(dt p) r -> p dt r", p=128))
            p.dma("gpsimd", bscg, adA_sb[:, 2, :, :], vA_i.rearrange("(dt p) r -> p dt r", p=128))
            p.dma("gpsimd", bscg, adB_sb[:, 0, :], qB_i)
            p.dma("gpsimd", bscg, adB_sb[0:16, 1, 0:KVD], kB_i)
            p.dma("gpsimd", bscg, adB_sb[0:16, 2, 0:KVD], vB_i)
            for eng in ("tensor", "vector", "scalar"):
                p.wait(eng, bsc, bsc.n)
                p.wait(eng, bscg, bscg.n)

            x2_t = x2T.rearrange("(dt p) t -> p dt t", p=128)
            bch = [p.ch("bsl0"), p.ch("bsl1")]
            xbch = p.ch("xbc")
            pub_ch = p.ch("pubb")
            bfree = [0, 0]
            xb_free = 0
            PROJ = [("q", wqb_i, 16, True), ("k", wkb_i, 4, False), ("v", wvb_i, 4, False)]
            for tb in range(NTB):
                if xb_free:
                    p.wait("sync", dve, xb_free)
                p.dma("sync", xbch, xb_sb[:, :, :], x2_t[:, :, tb * TB:(tb + 1) * TB])
                p.wait("vector", xbch, xbch.n)
                ln_done = emit_layernorm(p, xb_sb, C["xcast"], C)
                xb_free = ln_done
                p.wait("tensor", dve, ln_done)
                for ai in range(3):
                    if ai > 0:
                        p.wait("tensor", act, act.n)
                    for dt in range(DT_PER):
                        p.op("tensor", lambda e, ai=ai, dt=dt: e.matmul(
                            ut_ps[0:RANK, 0, :], adA_sb[:, ai, dt, :], C["xcast"][:, dt, :],
                            start=(dt == 0), stop=(dt == DT_PER - 1)),
                            inc=(pe if dt == DT_PER - 1 else None))
                    ad_mm = pe.n
                    p.wait("scalar", pe, ad_mm)
                    p.op("scalar", lambda e, ai=ai: e.copy(uad_sb[:, ai, :], ut_ps[0:RANK, 0, :]), inc=act)
                uad_done = act.n
                p.wait("tensor", act, uad_done)
                for name, wsrc, nch, do_gain in PROJ:
                    wt_t = wsrc.rearrange("(dt p) m -> p dt m", p=128)
                    ai = {"q": 0, "k": 1, "v": 2}[name]
                    for hc in range(nch):
                        bsl = hc % 2
                        psl = hc % 2
                        if bfree[bsl]:
                            p.wait("gpsimd", pe, bfree[bsl])
                        p.dma("gpsimd", bch[bsl], bslab[:, bsl, :, :], wt_t[:, :, hc * 128:(hc + 1) * 128])
                        p.wait("tensor", bch[bsl], bch[bsl].n)
                        p.wait("tensor", act, act.n)
                        p.wait("tensor", dve, dve.n)
                        for dt in range(DT_PER):
                            p.op("tensor", lambda e, bsl=bsl, psl=psl, dt=dt: e.matmul(
                                bj_ps[:, psl, :], bslab[:, bsl, dt, :], C["xcast"][:, dt, :],
                                start=(dt == 0), stop=False))
                        p.op("tensor", lambda e, ai=ai, hc=hc, psl=psl: e.matmul(
                            bj_ps[:, psl, :], adB_sb[:, ai, hc * 128:(hc + 1) * 128],
                            uad_sb[:, ai, :], start=False, stop=True), inc=pe)
                        raw_mm = pe.n
                        bfree[bsl] = raw_mm
                        if name == "v":
                            p.wait("scalar", pe, raw_mm)
                            p.wait("scalar", pub_ch, pub_ch.n)
                            p.op("scalar", lambda e, psl=psl: e.copy(qn_sb[:, :], bj_ps[:, psl, :]), inc=act)
                            vev = act.n
                            p.wait("sync", act, vev)
                            p.dma("sync", pub_ch, vbT_own[hc, :, tb * TB:(tb + 1) * TB], qn_sb[:, :])
                            p.wait("tensor", act, vev)
                            for qq in range(4):
                                p.op("tensor", lambda e, qq=qq: e.transpose(
                                    tr_ps[:, qq, :], qn_sb[:, qq * 128:(qq + 1) * 128],
                                    ident_bf[:, :]), inc=(pe if qq == 3 else None))
                            trd = pe.n
                            p.wait("scalar", pe, trd)
                            p.op("scalar", lambda e: e.copy(vtb_sb[:, :, :], tr_ps[:, :, :]), inc=act)
                            p.wait("sync", act, act.n)
                            p.dma("sync", pub_ch,
                                  vb_pub[hc, tb * TB:(tb + 1) * TB, :].rearrange("(q t) d -> t q d", t=128),
                                  vtb_sb[:, :, :])
                            continue
                        p.wait("scalar", pe, raw_mm)
                        p.op("scalar", lambda e, psl=psl: e.copy(qraw_sb[:, :], bj_ps[:, psl, :]), inc=act)
                        p.wait("vector", act, act.n)
                        p.op("vector", lambda e: e.tensor_tensor(sqq_sb[:, :], qraw_sb[:, :], qraw_sb[:, :], ALU.mult), inc=dve)
                        p.wait("tensor", dve, dve.n)
                        p.op("tensor", lambda e: e.matmul(ut_ps[0:1, 0, :], invh_col[:, :], sqq_sb[:, :],
                                                          start=True, stop=True), inc=pe)
                        p.wait("scalar", pe, pe.n)
                        p.op("scalar", lambda e: e.activation(mrow[:, :], ut_ps[0:1, 0, :], AF.Ln,
                                                              bias=C["epsr1"], scale=1.0), inc=act)
                        p.op("scalar", lambda e: e.activation(mrow[:, :], mrow[:, :], AF.Exp,
                                                              bias=C["zero1"], scale=-0.5), inc=act)
                        rstd_done = act.n
                        if do_gain:
                            p.wait("vector", act, rstd_done)
                            p.op("vector", lambda e, hc=hc: e.tensor_scalar(
                                mrow[:, :], mrow[:, :], qg_sb[0:1, hc:hc + 1], None, ALU.mult), inc=dve)
                            p.wait("tensor", dve, dve.n)
                        else:
                            p.wait("tensor", act, rstd_done)
                        p.op("tensor", lambda e: e.matmul(ut_ps[:, 1, :], ones_f32[:, :], mrow[:, :],
                                                          start=True, stop=True), inc=pe)
                        gb_mm = pe.n
                        p.wait("vector", pe, gb_mm)
                        p.wait("vector", pub_ch, pub_ch.n)
                        p.op("vector", lambda e: e.tensor_tensor(qn_sb[:, :], qraw_sb[:, :], ut_ps[:, 1, :], ALU.mult), inc=dve)
                        qn_done = dve.n
                        p.wait("tensor", dve, qn_done)
                        p.op("tensor", lambda e: e.matmul(ut_ps[:, 1, :], jt_sb[:, :], qn_sb[:, :],
                                                          start=True, stop=True), inc=pe)
                        jq_mm = pe.n
                        p.wait("vector", pe, jq_mm)
                        p.wait("vector", pub_ch, pub_ch.n)
                        p.op("vector", lambda e, tb=tb: e.tensor_tensor(
                            r1_sb[:, :], qn_sb[:, :], cos_sb[:, tb * TB:(tb + 1) * TB], ALU.mult))
                        p.op("vector", lambda e, tb=tb: e.tensor_tensor(
                            ln_t[:, :], ut_ps[:, 1, :], sin_sb[:, tb * TB:(tb + 1) * TB], ALU.mult))
                        rsl = hc % 2
                        p.op("vector", lambda e, rsl=rsl: e.tensor_tensor(
                            rot_sb[:, rsl, :], r1_sb[:, :], ln_t[:, :], ALU.add), inc=dve)
                        rot_done = dve.n
                        p.wait("sync", dve, rot_done)
                        dst = qb_pub if name == "q" else kb_pub
                        p.dma("sync", pub_ch, dst[hc, :, tb * TB:(tb + 1) * TB], rot_sb[:, rsl, :])

        for chn in list(p.chs.values()):
            if chn.n and chn.name not in ("pe", "dve", "act", "gp"):
                p.wait("gpsimd", chn, chn.n)
        p.finalize(block)
    return nc


def run_L2(inputs, res1):
    x0 = _f32(inputs["L0_attn_a_x"])
    if "L2" not in _build_cache:
        _build_cache["L2"] = build_L2()
    nc2 = _build_cache["L2"]

    r = res1.results
    r_l1 = r
    pubK = [r[c]["k_pub"] for c in range(NCORE)]
    pubV = [r[c]["v_pub"] for c in range(NCORE)]
    wg_full = np.concatenate([r1c["w_gate"] for r1c in r_l1], axis=1)
    wu_full = np.concatenate([r1c["w_up"] for r1c in r_l1], axis=1)
    wd_full = np.concatenate([r1c["w_down"] for r1c in r_l1], axis=1)
    _build_cache["wg_full"] = wg_full
    _build_cache["wu_full"] = wu_full
    _build_cache["wd_full"] = wd_full
    wo_l = np.ascontiguousarray(_f32(inputs["L0_attn_a_W_O"]).T)
    cvals = host_consts_vals()
    hc0 = host_constants(0)

    in_maps = []
    for c in range(NCORE):
        pos = core_positions(c)
        b = c // 4
        wA, wB, _ = attn_unit_table(c)
        kk, vv = host_kv_slots(c, pubK, pubV, H_A)
        cosT, sinT = rope_tables(pos)
        in_maps.append(dict(
            xT=np.ascontiguousarray(x0[b].T[:, pos]),
            qa_i=r[c]["q_pub"], ka_own=r[c]["k_pub"], va_own=r[c]["v_pub"],
            ka_sl=kk, va_sl=vv,
            uw_i=np.stack([np.broadcast_to(wA, (128, 9)),
                           np.broadcast_to(wB, (128, 9))]).astype(np.float32),
            masks_i=host_diag_masks(),
            wo_l=wo_l, wg_i=wg_full, wu_i=wu_full, wd_i=wd_full,
            wqb_i=_f32(inputs["L0_attn_b_Wq"]), wkb_i=_f32(inputs["L0_attn_b_Wk"]),
            wvb_i=_f32(inputs["L0_attn_b_Wv"]),
            qA_i=_f32(inputs["L0_attn_b_q_A"]), qB_i=_f32(inputs["L0_attn_b_q_B"]),
            kA_i=_f32(inputs["L0_attn_b_k_A"]), kB_i=_f32(inputs["L0_attn_b_k_B"]),
            vA_i=_f32(inputs["L0_attn_b_v_A"]), vB_i=_f32(inputs["L0_attn_b_v_B"]),
            qgain_i=_f32(inputs["L0_attn_b_q_gain"]).reshape(1, H_A),
            cos_i=cosT, sin_i=sinT, jt_i=rope_J(),
            ident=hc0["ident"], ones_row=hc0["ones_row"],
            invd_col_i=hc0["invd_col"],
            invh_col_i=np.full((128, 1), 1.0 / HD, np.float32),
            onesc_i=np.ones((128, 1), np.float32),
            consts_i=cvals,
        ))
    return run_bass_kernel_spmd(nc2, in_maps, core_ids=list(range(NCORE)))


def build_L3():
    nc = bass.Bass("TRN2", target_bir_lowering=False, num_devices=NCORE)

    x2T = nc.dram_tensor("x2T", [D, TOK], F32, kind="ExternalInput").ap()
    qb_i = nc.dram_tensor("qb_i", [H_A, HD, TOK], BF16, kind="ExternalInput").ap()
    kb_own = nc.dram_tensor("kb_own", [HKV, HD, TOK], BF16, kind="ExternalInput").ap()
    vb_own = nc.dram_tensor("vb_own", [HKV, TOK, HD], BF16, kind="ExternalInput").ap()
    kb_sl = nc.dram_tensor("kb_sl", [HKV, 7, HD, CHUNK], BF16, kind="ExternalInput").ap()
    vb_sl = nc.dram_tensor("vb_sl", [HKV, 7, CHUNK, HD], BF16, kind="ExternalInput").ap()
    vbT_i = nc.dram_tensor("vbT_i", [HKV, HD, TOK], BF16, kind="ExternalInput").ap()
    uw_i = nc.dram_tensor("uw_i", [2, 128, 9], F32, kind="ExternalInput").ap()
    masks_i = nc.dram_tensor("masks_i", [128, 8, 2, 512], F32, kind="ExternalInput").ap()
    wp_i = nc.dram_tensor("wp_i", [D, D], F32, kind="ExternalInput").ap()
    pA_i = nc.dram_tensor("pA_i", [D, RANK], F32, kind="ExternalInput").ap()
    pB_i = nc.dram_tensor("pB_i", [RANK, D], F32, kind="ExternalInput").ap()
    qgain_i = nc.dram_tensor("qgain_i", [1, H_A], F32, kind="ExternalInput").ap()
    wg_i = nc.dram_tensor("wg_i", [D, INTER], BF16, kind="ExternalInput").ap()
    wu_i = nc.dram_tensor("wu_i", [D, INTER], BF16, kind="ExternalInput").ap()
    wd_i = nc.dram_tensor("wd_i", [INTER, D], BF16, kind="ExternalInput").ap()
    ones_i = nc.dram_tensor("ones_row", [1, 128], F32, kind="ExternalInput").ap()
    invd_i = nc.dram_tensor("invd_col_i", [128, 1], F32, kind="ExternalInput").ap()
    onesc_i = nc.dram_tensor("onesc_i", [128, 1], F32, kind="ExternalInput").ap()
    consts_i = nc.dram_tensor("consts_i", [128, 8], F32, kind="ExternalInput").ap()

    outT = nc.dram_tensor("outT", [D, TOK], F32, kind="ExternalOutput").ap()
    resid_d = nc.dram_tensor("resid_d", [D, TOK], F32)

    with ExitStack() as ctx:
        e = ctx.enter_context
        invd_col = e(nc.sbuf_tensor("invd_col", [128, 1], BF16))
        onesb_col = e(nc.sbuf_tensor("onesb_col", [128, 1], BF16))
        ones_f32 = e(nc.sbuf_tensor("ones_f32", [1, 128], F32))
        const_sb = e(nc.sbuf_tensor("const_sb", [128, 8], F32))
        uw_sb = e(nc.sbuf_tensor("uw_sb", [128, 2, 9], F32))
        masks_sb = e(nc.sbuf_tensor("masks_sb", [128, 8, 2, 512], BF16))
        qg_sb = e(nc.sbuf_tensor("qg_sb", [1, H_A], F32))

        block = e(nc.Block())
        p = Prog(nc, ctx)
        pe, dve, act = p.ch("pe"), p.ch("dve"), p.ch("act")
        gc, sc = p.ch("gconst"), p.ch("sconst")

        C = dict(invd_col=invd_col[:, :], ones_f32=ones_f32[:, :],
                 eps1=const_sb[0:1, 1:2], zero1=const_sb[0:1, 0:1],
                 zero128=const_sb[:, 0:1],
                 epsr1=const_sb[0:1, 2:3], tiny1=const_sb[0:1, 4:5])

        p.dma("gpsimd", gc, invd_col[:, :], invd_i)
        p.dma("gpsimd", gc, onesb_col[:, :], onesc_i)
        p.dma("gpsimd", gc, masks_sb[:, :, :, :], masks_i)
        p.dma("sync", sc, ones_f32[:, :], ones_i)
        p.dma("sync", sc, const_sb[:, :], consts_i)
        p.dma("sync", sc, uw_sb[:, 0, :], uw_i[0, :, :])
        p.dma("sync", sc, uw_sb[:, 1, :], uw_i[1, :, :])
        p.dma("sync", sc, qg_sb[:, :], qgain_i)
        for eng in ("tensor", "vector", "scalar"):
            p.wait(eng, gc, gc.n)
            p.wait(eng, sc, sc.n)

        with ExitStack() as s1:
            e1 = s1.enter_context
            z_buf = e1(nc.sbuf_tensor("z_buf", [128, H_A, TOK], BF16))
            qa_sb = e1(nc.sbuf_tensor("qa_sb", [128, 2, TOK], BF16))
            kk_sb = e1(nc.sbuf_tensor("kk_sb", [128, 2, CHUNK], BF16))
            vv_sb = e1(nc.sbuf_tensor("vv_sb", [128, 2, 8, 128], BF16))
            qu_sb = e1(nc.sbuf_tensor("qu_sb", [128, CHUNK], BF16))
            pt_sb = e1(nc.sbuf_tensor("pt_sb", [128, 2, CHUNK], BF16))
            yA = e1(nc.sbuf_tensor("yA", [128, CHUNK], F32))
            yB = e1(nc.sbuf_tensor("yB", [128, CHUNK], F32))
            dAr = e1(nc.sbuf_tensor("dAr", [1, CHUNK], F32))
            dBr = e1(nc.sbuf_tensor("dBr", [1, CHUNK], F32))
            rrow = e1(nc.sbuf_tensor("rrow", [1, 2, CHUNK], F32))
            bias_sb = e1(nc.sbuf_tensor("bias_sb", [128, H_A], F32))
            grow = e1(nc.sbuf_tensor("grow", [1, H_A], F32))
            vbT_sb = e1(nc.sbuf_tensor("vbT_sb", [128, HKV, TOK], BF16))
            vn_sb = e1(nc.sbuf_tensor("vn_sb", [128, HKV, TOK], BF16))
            sqv_sb = e1(nc.sbuf_tensor("sqv_sb", [128, TOK], BF16))
            vrow = e1(nc.sbuf_tensor("vrow", [1, TOK], F32))
            tdot = e1(nc.sbuf_tensor("tdot", [128, TOK], F32))
            st_ps = e1(nc.psum_tensor("st_ps", [128, 2, 512], F32))
            y_ps = e1(nc.psum_tensor("y_ps", [128, 2, 512], F32))
            dr_ps = e1(nc.psum_tensor("dr_ps", [128, 2, 512], F32))

            vbc = p.ch("vbc")
            p.dma("gpsimd", vbc, vbT_sb[:, :, :], vbT_i.rearrange("k h t -> h k t"))
            p.wait("vector", vbc, vbc.n)
            p.wait("tensor", vbc, vbc.n)

            p.op("vector", lambda e: e.tensor_scalar(grow[:, :], qg_sb[:, :], -1.0, None, ALU.mult))
            p.op("vector", lambda e: e.tensor_tensor(grow[:, :], grow[:, :], qg_sb[:, :], ALU.max))
            p.op("vector", lambda e: e.tensor_scalar(grow[:, :], grow[:, :], -11.313708498984761, None, ALU.mult), inc=dve)
            p.wait("tensor", dve, dve.n)
            for h in range(H_A):
                p.wait("tensor", act, act.n)
                p.op("tensor", lambda e, h=h: e.matmul(st_ps[:, 0, 0:1], ones_f32[:, :], grow[:, h:h + 1],
                                                       start=True, stop=True), inc=pe)
                p.wait("scalar", pe, pe.n)
                p.op("scalar", lambda e, h=h: e.copy(bias_sb[:, h:h + 1], st_ps[:, 0, 0:1]), inc=act)
            p.wait("tensor", act, act.n)

            for kv in range(HKV):
                p.op("vector", lambda e, kv=kv: e.tensor_tensor(
                    sqv_sb[:, :], vbT_sb[:, kv, :], vbT_sb[:, kv, :], ALU.mult), inc=dve)
                p.wait("tensor", dve, dve.n)
                for jp in range(2):
                    p.wait("tensor", act, act.n)
                    for j in (2 * jp, 2 * jp + 1):
                        p.op("tensor", lambda e, j=j: e.matmul(
                            dr_ps[0:1, j % 2, :], onesb_col[:, :], sqv_sb[:, j * 512:(j + 1) * 512],
                            start=True, stop=True), inc=(pe if j % 2 == 1 else None))
                    p.wait("scalar", pe, pe.n)
                    p.op("scalar", lambda e, jp=jp: e.copy(vrow[:, jp * 1024:(jp + 1) * 1024],
                                                           dr_ps[0:1, :, :].rearrange("p a b -> p (a b)")), inc=act)
                p.wait("scalar", act, act.n)
                p.op("scalar", lambda e: e.activation(vrow[:, :], vrow[:, :], AF.Ln, bias=C["tiny1"], scale=1.0), inc=act)
                p.op("scalar", lambda e: e.activation(vrow[:, :], vrow[:, :], AF.Exp, bias=C["zero1"], scale=-0.5), inc=act)
                p.wait("tensor", act, act.n)
                for j in range(4):
                    p.wait("tensor", dve, dve.n)
                    p.op("tensor", lambda e, j=j: e.matmul(
                        st_ps[:, j % 2, :], ones_f32[:, :], vrow[:, j * 512:(j + 1) * 512],
                        start=True, stop=True), inc=pe)
                    p.wait("vector", pe, pe.n)
                    p.op("vector", lambda e, kv=kv, j=j: e.tensor_tensor(
                        vn_sb[:, kv, j * 512:(j + 1) * 512], vbT_sb[:, kv, j * 512:(j + 1) * 512],
                        st_ps[:, j % 2, :], ALU.mult), inc=dve)
                p.wait("tensor", dve, dve.n)

            qch = [p.ch("qb0"), p.ch("qb1")]
            kch = [p.ch("kb0"), p.ch("kb1")]
            vch = [p.ch("vb0"), p.ch("vb1")]
            A = dict(qa_sb=qa_sb, kk_sb=kk_sb, vv_sb=vv_sb, qu_sb=qu_sb,
                     pt_sb=pt_sb, yA=yA, yB=yB, dA=dAr, dB=dBr, rrow=rrow,
                     st_ps=st_ps, y_ps=y_ps, den_ps=dr_ps[0:1, :, :], rc_ps=dr_ps,
                     masks=masks_sb, z_buf=z_buf,
                     uwA=uw_sb[:, 0, :], uwB=uw_sb[:, 1, :],
                     onesb=onesb_col[:, :], ones_f32=ones_f32[:, :],
                     q_ch=qch, q_n=[0, 0], kv_chk=kch, kv_chv=vch,
                     kv_nk=[0, 0], kv_nv=[0, 0],
                     q_eng="sync", kv_eng="gpsimd",
                     zmap=lambda h: h,
                     bias_col=lambda h: bias_sb[:, h:h + 1])

            def load_q(h, slot):
                p.dma("sync", qch[slot], qa_sb[:, slot, :], qb_i[h, :, :])
                A["q_n"][slot] = qch[slot].n
            A["load_q"] = load_q

            def load_kv(h, u, slot):
                kv = h // 4
                if u == 0:
                    ksrc, vsrc = kb_own[kv, :, 0:CHUNK], vb_own[kv, 0:CHUNK, :]
                elif u == 1:
                    ksrc, vsrc = kb_own[kv, :, CHUNK:TOK], vb_own[kv, CHUNK:TOK, :]
                else:
                    ksrc, vsrc = kb_sl[kv, u - 2, :, :], vb_sl[kv, u - 2, :, :]
                p.dma("gpsimd", kch[slot], kk_sb[:, slot, :], ksrc)
                A["kv_nk"][slot] = kch[slot].n
                p.dma("gpsimd", vch[slot], vv_sb[:, slot, :, :],
                      vsrc.rearrange("(c p) d -> p c d", p=128))
                A["kv_nv"][slot] = vch[slot].n
            A["load_kv"] = load_kv

            z_done = emit_attn_core(p, A, H_A, 0.08838834764831845)

            p.wait("vector", dve, z_done)
            for h in range(H_A):
                kv = h // 4
                p.op("vector", lambda e, h=h, kv=kv: e.tensor_tensor(
                    tdot[:, :], z_buf[:, h, :], vn_sb[:, kv, :], ALU.mult), inc=dve)
                p.op("vector", lambda e: e.tensor_copy(sqv_sb[:, :], tdot[:, :]), inc=dve)
                p.wait("tensor", dve, dve.n)
                for jp in range(2):
                    p.wait("tensor", act, act.n)
                    for j in (2 * jp, 2 * jp + 1):
                        p.op("tensor", lambda e, j=j: e.matmul(
                            dr_ps[0:1, j % 2, :], onesb_col[:, :], sqv_sb[:, j * 512:(j + 1) * 512],
                            start=True, stop=True), inc=(pe if j % 2 == 1 else None))
                    p.wait("scalar", pe, pe.n)
                    p.op("scalar", lambda e, jp=jp: e.copy(vrow[:, jp * 1024:(jp + 1) * 1024],
                                                           dr_ps[0:1, :, :].rearrange("p a b -> p (a b)")), inc=act)
                p.wait("tensor", act, act.n)
                for j in range(4):
                    p.wait("tensor", dve, dve.n)
                    p.op("tensor", lambda e, j=j: e.matmul(
                        st_ps[:, j % 2, :], ones_f32[:, :], vrow[:, j * 512:(j + 1) * 512],
                        start=True, stop=True), inc=pe)
                    p.wait("vector", pe, pe.n)
                    p.op("vector", lambda e, kv=kv, j=j: e.tensor_tensor(
                        tdot[:, j * 512:(j + 1) * 512], st_ps[:, j % 2, :],
                        vn_sb[:, kv, j * 512:(j + 1) * 512], ALU.mult), inc=dve)
                p.op("vector", lambda e, h=h: e.tensor_tensor(
                    z_buf[:, h, :], z_buf[:, h, :], tdot[:, :], ALU.subtract), inc=dve)
                p.wait("tensor", dve, dve.n)

            woslab = e1(nc.sbuf_tensor("woslab", [128, 2, DT_PER, 128], BF16))
            adA_sb = e1(nc.sbuf_tensor("adA_sb", [128, DT_PER, RANK], BF16))
            adB_sb = e1(nc.sbuf_tensor("adB_sb", [16, D], BF16))
            uad_sb = e1(nc.sbuf_tensor("uad_sb", [16, TB], BF16))
            xin1 = e1(nc.sbuf_tensor("xin1", [128, TB], F32))
            wp_t = wp_i.rearrange("(ft p) m -> p ft m", p=128)
            x2_t = x2T.rearrange("(dt p) t -> p dt t", p=128)
            rd_t = resid_d.rearrange("(dt p) t -> p dt t", p=128)
            adc = p.ch("adc")
            p.dma("gpsimd", adc, adA_sb[:, :, :], pA_i.rearrange("(dt p) r -> p dt r", p=128))
            p.dma("gpsimd", adc, adB_sb[:, :], pB_i)
            p.wait("tensor", adc, adc.n)
            wo_ch = [p.ch("wo0"), p.ch("wo1")]
            x1_ch = p.ch("x1c")
            xi_ch = p.ch("xic")
            wo_free = [0, 0]
            uad_all = e1(nc.sbuf_tensor("uad_all", [16, TOK], BF16))
            for tb in range(NTB):
                p.wait("tensor", act, act.n)
                for ft in range(DT_PER):
                    p.op("tensor", lambda e, ft=ft, tb=tb: e.matmul(
                        y_ps[0:RANK, 0, :], adA_sb[:, ft, :],
                        z_buf[:, ft, tb * TB:(tb + 1) * TB],
                        start=(ft == 0), stop=(ft == DT_PER - 1)),
                        inc=(pe if ft == DT_PER - 1 else None))
                p.wait("scalar", pe, pe.n)
                p.op("scalar", lambda e, tb=tb: e.copy(uad_all[:, tb * TB:(tb + 1) * TB],
                                                       y_ps[0:RANK, 0, :]), inc=act)
            uad_done = act.n
            p.wait("tensor", act, uad_done)
            ps_free = [0, 0]
            for dt in range(DT_PER):
                wsl = dt % 2
                if wo_free[wsl]:
                    p.wait("gpsimd", pe, wo_free[wsl])
                p.dma("gpsimd", wo_ch[wsl], woslab[:, wsl, :, :], wp_t[:, :, dt * 128:(dt + 1) * 128])
                p.wait("tensor", wo_ch[wsl], wo_ch[wsl].n)
                for tb in range(NTB):
                    psl = tb % 2
                    if ps_free[psl]:
                        p.wait("tensor", dve, ps_free[psl])
                    for ft in range(DT_PER):
                        p.op("tensor", lambda e, wsl=wsl, psl=psl, ft=ft, tb=tb: e.matmul(
                            st_ps[:, psl, :], woslab[:, wsl, ft, :],
                            z_buf[:, ft, tb * TB:(tb + 1) * TB],
                            start=(ft == 0), stop=False))
                    p.op("tensor", lambda e, wsl=wsl, psl=psl, dt=dt, tb=tb: e.matmul(
                        st_ps[:, psl, :], adB_sb[:, dt * 128:(dt + 1) * 128],
                        uad_all[:, tb * TB:(tb + 1) * TB],
                        start=False, stop=True), inc=pe)
                    wo_mm = pe.n
                    p.wait("sync", dve, dve.n)
                    p.wait("sync", x1_ch, x1_ch.n)
                    p.dma("sync", xi_ch, xin1[:, :], x2_t[:, dt, tb * TB:(tb + 1) * TB])
                    p.wait("vector", xi_ch, xi_ch.n)
                    p.wait("vector", pe, wo_mm)
                    p.op("vector", lambda e, psl=psl: e.tensor_tensor(
                        xin1[:, :], xin1[:, :], st_ps[:, psl, :], ALU.add), inc=dve)
                    ps_free[psl] = dve.n
                    p.wait("sync", dve, dve.n)
                    p.dma("sync", x1_ch, rd_t[:, dt, tb * TB:(tb + 1) * TB], xin1[:, :])
                wo_free[wsl] = pe.n

        _barrier(p)

        with ExitStack() as s2:
            e2 = s2.enter_context
            at_sb = e2(nc.sbuf_tensor("at_sb", [128, IT_PER, TB], BF16))
            gslab = e2(nc.sbuf_tensor("gslab", [128, 2, 2, DT_PER, 128], BF16))
            dslab = e2(nc.sbuf_tensor("dslab", [128, 2, IT_PER, 128], BF16))
            xin_sb = e2(nc.sbuf_tensor("xin_sb", [128, DT_PER, TB], F32))
            sil_sb = e2(nc.sbuf_tensor("sil_sb", [128, TB], BF16))
            gu_ps = e2(nc.psum_tensor("gu_ps", [128, 2, TB], F32))
            dn_ps = e2(nc.psum_tensor("dn_ps", [128, 2, TB], F32))
            psA2 = e2(nc.psum_tensor("psA2", [128, 2 * TB], F32))
            bcA2 = e2(nc.psum_tensor("bcA2", [128, 2, TB], F32))
            C["psA"] = psA2
            C["bcA"] = bcA2
            C["xcast"] = e2(nc.sbuf_tensor("xcast", [128, DT_PER, TB], BF16))
            C["xsq"] = e2(nc.sbuf_tensor("xsq", [128, DT_PER, TB], BF16))
            C["rows4"] = e2(nc.sbuf_tensor("rows4", [1, 4, TB], F32))
            C["ln_t"] = e2(nc.sbuf_tensor("ln_t", [128, TB], F32))
            Fd = dict(wg=wg_i, wu=wu_i, wd=wd_i,
                      gslab=gslab, dslab=dslab, at_sb=at_sb,
                      xin_sb=xin_sb, xout_sb=xin_sb,
                      gu_ps=gu_ps, dn_ps=dn_ps, sil_sb=sil_sb,
                      ch_g=[p.ch("ffg0"), p.ch("ffg1")],
                      ch_d=[p.ch("ffd0"), p.ch("ffd1")],
                      ch_xi=p.ch("ffxi"), ch_xo=p.ch("ffxo"),
                      g_eng="gpsimd", d_eng="gpsimd", xi_eng="sync", xo_eng="sync")
            emit_ffn(p, Fd, resid_d, outT, C)

        for chn in list(p.chs.values()):
            if chn.n and chn.name not in ("pe", "dve", "act", "gp"):
                p.wait("gpsimd", chn, chn.n)
        p.finalize(block)
    return nc


def run_L3(inputs, res2):
    if "L3" not in _build_cache:
        _build_cache["L3"] = build_L3()
    nc3 = _build_cache["L3"]
    r = res2.results
    pubK = [r[c]["kb_pub"] for c in range(NCORE)]
    pubV = [r[c]["vb_pub"] for c in range(NCORE)]
    cvals = host_consts_vals()
    hc0 = host_constants(0)
    wg_full = _build_cache["wg_full"]
    wu_full = _build_cache["wu_full"]
    wd_full = _build_cache["wd_full"]

    in_maps = []
    for c in range(NCORE):
        wA, wB, _ = attn_unit_table(c)
        kk, vv = host_kv_slots(c, pubK, pubV, HKV)
        in_maps.append(dict(
            x2T=r[c]["x2T"],
            qb_i=r[c]["qb_pub"], kb_own=r[c]["kb_pub"], vb_own=r[c]["vb_pub"],
            kb_sl=kk, vb_sl=vv, vbT_i=r[c]["vbT_own"],
            uw_i=np.stack([np.broadcast_to(wA, (128, 9)),
                           np.broadcast_to(wB, (128, 9))]).astype(np.float32),
            masks_i=host_diag_masks(),
            wp_i=_f32(inputs["L0_attn_b_Wproj"]),
            pA_i=_f32(inputs["L0_attn_b_p_A"]), pB_i=_f32(inputs["L0_attn_b_p_B"]),
            qgain_i=_f32(inputs["L0_attn_b_q_gain"]).reshape(1, H_A),
            wg_i=wg_full, wu_i=wu_full, wd_i=wd_full,
            ones_row=hc0["ones_row"], invd_col_i=hc0["invd_col"],
            onesc_i=np.ones((128, 1), np.float32),
            consts_i=cvals,
        ))
    return run_bass_kernel_spmd(nc3, in_maps, core_ids=list(range(NCORE)))


# version 02 (final)
# speedup vs baseline: 1.2086x; 1.2086x over previous
import math
from contextlib import ExitStack

import numpy as np
import ml_dtypes

import concourse.bass as bass
import concourse.mybir as mybir
from concourse.bass_utils import run_bass_kernel_spmd

F32 = mybir.dt.float32
BF16 = mybir.dt.bfloat16
AF = mybir.ActivationFunctionType
ALU = mybir.AluOpType

BF = ml_dtypes.bfloat16

B = 2
S = 8192
D = 2048
NCORE = 8
CHUNK = 1024
NCH = S // CHUNK
TOK = 2 * CHUNK
H_A = 16
HD = 128
HKV = 4
KVD = HKV * HD
RANK = 16
ROPE_DIMS = 64
ROPE_BASE = 10000.0
N_CTRL = 130156
INTER = 8192
NW = INTER * D
LN_EPS = 1e-5
RMS_EPS = float(np.finfo(np.float32).eps)
DT_PER = D // 128
IT_PER = INTER // 128
TB = 512
NTB = TOK // TB

WTILES = 128
SH_SAMP = NW // NCORE
SH_ROWS = SH_SAMP // 128
GATE_IB = SH_SAMP // D
DOWN_HB = SH_SAMP // INTER


_cache = {}


def spline_tables():
    if "sp" not in _cache:
        t = np.linspace(0.0, N_CTRL - 1.0, NW, dtype=np.float64)
        i = np.clip(np.floor(t).astype(np.int64), 0, N_CTRL - 2)
        f = (t - i).astype(np.float32)
        i_r = i.reshape(-1, 128)
        f_r = f.reshape(-1, 128)
        i0 = i_r[:, 0]
        d = i_r - i0[:, None]
        assert d.min() >= 0 and d.max() <= 1
        nostep = d.max(axis=1) == 0
        cb = np.where(nostep, 128, np.argmax(d == 1, axis=1)).astype(np.float32)
        _cache["sp"] = (i0, cb, f_r[:, 0])
    return _cache["sp"]


def host_constants(core):
    i0, cb, f0 = spline_tables()
    r0 = core * SH_ROWS
    f0_t = np.empty((3, 128, WTILES), np.float32)
    cb_t = np.empty((3, 128, WTILES), np.float32)
    for w in range(3):
        f0_t[w] = f0[r0:r0 + SH_ROWS].reshape(WTILES, 128).T
        cb_t[w] = cb[r0:r0 + SH_ROWS].reshape(WTILES, 128).T
    delta32 = np.float32((N_CTRL - 1.0) / (NW - 1.0))
    cdelta = np.broadcast_to(
        (np.arange(128, dtype=np.float32) * delta32)[None, :], (128, 128)).copy()
    iota_c = np.broadcast_to(
        np.arange(128, dtype=np.float32)[None, :], (128, 128)).copy()
    return dict(f0cb=np.stack([f0_t, cb_t], axis=1),
                cdelta=cdelta, iota=iota_c,
                ident=np.eye(128, dtype=np.float32),
                ones_row=np.ones((1, 128), np.float32),
                invd_col=np.full((128, 1), 1.0 / D, np.float32))


def host_gwin(core, cps):
    i0, _, _ = spline_tables()
    r0 = core * SH_ROWS
    out = np.empty((3, 128, WTILES * 5), np.float32)
    idx = i0[r0:r0 + SH_ROWS].reshape(WTILES, 128)
    for w, cp in enumerate(cps):
        cp_pad = np.concatenate([cp[:1], cp, cp[-1:], cp[-1:], cp[-1:]])
        g = cp_pad[idx[:, :, None] + np.arange(5)[None, None, :]]
        out[w] = np.transpose(g, (1, 0, 2)).reshape(128, WTILES * 5)
    return out


def host_consts_vals():
    v = np.zeros((128, 8), np.float32)
    v[:, 1] = LN_EPS
    v[:, 2] = RMS_EPS
    v[:, 3] = -20.0
    v[:, 4] = 1e-30
    return v


def rope_tables(pos):
    inv = 1.0 / ROPE_BASE ** (np.arange(0, ROPE_DIMS, 2, dtype=np.float32) / ROPE_DIMS)
    fr = pos[None, :].astype(np.float32) * inv[:, None]
    cos = np.cos(fr).astype(np.float32)
    sin = np.sin(fr).astype(np.float32)
    T = pos.shape[0]
    cosT = np.ones((128, T), np.float32)
    sinT = np.zeros((128, T), np.float32)
    cosT[0:32] = cos
    cosT[32:64] = cos
    sinT[0:32] = sin
    sinT[32:64] = sin
    return cosT, sinT


def rope_J():
    J = np.zeros((128, 128), np.float32)
    for i in range(32):
        J[i, 32 + i] = 1.0
        J[32 + i, i] = -1.0
    return np.ascontiguousarray(J.T)


def host_diag_masks():
    m = np.zeros((128, 8, 2, 512), np.float32)
    for s in range(8):
        tk = s * 128 + np.arange(128)
        for h2 in range(2):
            tq = h2 * 512 + np.arange(512)
            m[:, s, h2, :] = (tk[:, None] <= tq[None, :]).astype(np.float32)
    return m


def core_chunks(core):
    g = core % 4
    return g, 7 - g


def core_positions(core):
    cA, cB = core_chunks(core)
    return np.concatenate([np.arange(cA * CHUNK, (cA + 1) * CHUNK),
                           np.arange(cB * CHUNK, (cB + 1) * CHUNK)])


ENGINES = ("tensor", "vector", "scalar", "sync", "gpsimd")


class Ch:
    __slots__ = ("h", "n", "name")

    def __init__(self, handle, name):
        self.h = handle
        self.n = 0
        self.name = name


class Prog:
    def __init__(self, nc, ctx):
        self.nc = nc
        self.ctx = ctx
        self.ops = {e: [] for e in ENGINES}
        self.chs = {}
        self.dma_pend = {e: [] for e in ENGINES}
        self.last_wait = {e: {} for e in ENGINES}

    def ch(self, name):
        if name not in self.chs:
            self.chs[name] = Ch(self.ctx.enter_context(self.nc.semaphore(name)), name)
        return self.chs[name]

    SELF_CH = {"vector": "dve", "scalar": "act", "gpsimd": "gp"}

    def op(self, eng, fn, inc=None, k=1):
        sname = self.SELF_CH.get(eng)
        if sname is not None:
            sch = self.ch(sname)
            if sch.n:
                self.wait(eng, sch, sch.n)
            if inc is None:
                inc = sch
        if inc is None:
            self.ops[eng].append(fn)
            return None
        inc.n += k
        n = inc.n

        def run(e, fn=fn, inc=inc, k=k):
            fn(e).then_inc(inc.h, k)
        self.ops[eng].append(run)
        return n

    def dma_op(self, eng, fn, inc, k=16):
        inc.n += k
        n = inc.n

        def run(e, fn=fn, inc=inc, k=k):
            fn(e).then_inc(inc.h, k)
        self.ops[eng].append(run)
        return n

    def wait(self, eng, ch, n):
        assert n <= ch.n, f"wait {ch.name} {n} > emitted {ch.n}"
        if self.last_wait[eng].get(ch.name, -1) >= n:
            return
        self.last_wait[eng][ch.name] = n
        self.ops[eng].append(lambda e, ch=ch, n=n: e.wait_ge(ch.h, n))

    def dma(self, eng, ch, out, in_):
        return self.dma_op(eng, lambda e, o=out, i=in_: e.dma_start(out=o, in_=i), ch, k=16)

    def finalize(self, block):
        for eng in ENGINES:
            ops = self.ops[eng]

            def body(e, ops=ops):
                for fn in ops:
                    fn(e)
            getattr(block, eng)(body)



def emit_layernorm(p, resid_ap, ln_out_ap, C, tb_n=TB):
    pe, dve, act = p.ch("pe"), p.ch("dve"), p.ch("act")
    xc, xq = C["xcast"], C["xsq"]
    psA, bcA = C["psA"], C["bcA"]
    rows = C["rows4"]

    for dt in range(DT_PER):
        p.op("vector", lambda e, d=dt: e.tensor_copy(xc[:, d, :tb_n], resid_ap[:, d, :]), inc=dve)
    cast_done = dve.n
    for dt in range(DT_PER):
        p.op("vector", lambda e, d=dt: e.tensor_tensor(xq[:, d, :tb_n], xc[:, d, :tb_n], xc[:, d, :tb_n], ALU.mult), inc=dve)
    sq_done = dve.n

    p.wait("tensor", dve, cast_done)
    for dt in range(DT_PER):
        p.op("tensor", lambda e, d=dt: e.matmul(psA[0:1, 0:tb_n], C["invd_col"], xc[:, d, :tb_n],
                                                start=(d == 0), stop=(d == DT_PER - 1)),
             inc=(pe if dt == DT_PER - 1 else None))
    p.wait("tensor", dve, sq_done)
    for dt in range(DT_PER):
        p.op("tensor", lambda e, d=dt: e.matmul(psA[0:1, TB:TB + tb_n], C["invd_col"], xq[:, d, :tb_n],
                                                start=(d == 0), stop=(d == DT_PER - 1)),
             inc=(pe if dt == DT_PER - 1 else None))
    pe_stats = pe.n

    p.wait("scalar", pe, pe_stats)
    p.op("scalar", lambda e: e.copy(rows[:, 0, :tb_n], psA[0:1, 0:tb_n]), inc=act)
    p.wait("vector", act, act.n)
    p.op("vector", lambda e: e.tensor_tensor(rows[:, 1, :tb_n], rows[:, 0, :tb_n], rows[:, 0, :tb_n], ALU.mult), inc=dve)
    p.op("vector", lambda e: e.tensor_tensor(rows[:, 1, :tb_n], psA[0:1, TB:TB + tb_n], rows[:, 1, :tb_n], ALU.subtract), inc=dve)
    p.wait("scalar", dve, dve.n)
    p.op("scalar", lambda e: e.activation(rows[:, 2, :tb_n], rows[:, 1, :tb_n], AF.Ln, bias=C["eps1"], scale=1.0), inc=act)
    p.op("scalar", lambda e: e.activation(rows[:, 2, :tb_n], rows[:, 2, :tb_n], AF.Exp, bias=C["zero1"], scale=-0.5), inc=act)
    rstd_done = act.n

    p.wait("tensor", act, rstd_done)
    p.op("tensor", lambda e: e.matmul(bcA[:, 0, :tb_n], C["ones_f32"], rows[:, 0, :tb_n], start=True, stop=True), inc=pe)
    p.op("tensor", lambda e: e.matmul(bcA[:, 1, :tb_n], C["ones_f32"], rows[:, 2, :tb_n], start=True, stop=True), inc=pe)
    pe_bc = pe.n

    p.wait("vector", pe, pe_bc)
    for dt in range(DT_PER):
        p.op("vector", lambda e, d=dt: e.tensor_tensor(C["ln_t"][:, :tb_n], resid_ap[:, d, :], bcA[:, 0, :tb_n], ALU.subtract), inc=dve)
        p.op("vector", lambda e, d=dt: e.tensor_tensor(ln_out_ap[:, d, :], C["ln_t"][:, :tb_n], bcA[:, 1, :tb_n], ALU.mult), inc=dve)
    return dve.n


def emit_recon_weight(p, T, w, nout_ap, kind, stag_ch, state):
    pe, dve = p.ch("pe"), p.ch("dve")
    g5 = T["g_all"][:, w, :].rearrange("p (t k) -> p t k", k=5)
    coef, cs = T["coef"], T["cscr"]
    f0_col = T["f0cb_s"][:, w, 0, :]
    cb_col = T["f0cb_s"][:, w, 1, :]

    def gk(k):
        return g5[:, :, k]

    def cf(i):
        return coef[:, i, :]

    steps = [
        lambda e: e.tensor_copy(cf(0), gk(1)),
        lambda e: e.tensor_tensor(cs[:, 0, :], gk(2), gk(0), ALU.subtract),
        lambda e: e.tensor_scalar(cf(1), cs[:, 0, :], 0.5, None, ALU.mult),
        lambda e: e.scalar_tensor_tensor(cs[:, 0, :], gk(1), -2.5, gk(0), ALU.mult, ALU.add),
        lambda e: e.scalar_tensor_tensor(cs[:, 0, :], gk(2), 2.0, cs[:, 0, :], ALU.mult, ALU.add),
        lambda e: e.scalar_tensor_tensor(cf(2), gk(3), -0.5, cs[:, 0, :], ALU.mult, ALU.add),
        lambda e: e.tensor_scalar(cs[:, 0, :], gk(0), -0.5, None, ALU.mult),
        lambda e: e.scalar_tensor_tensor(cs[:, 0, :], gk(1), 1.5, cs[:, 0, :], ALU.mult, ALU.add),
        lambda e: e.scalar_tensor_tensor(cs[:, 0, :], gk(2), -1.5, cs[:, 0, :], ALU.mult, ALU.add),
        lambda e: e.scalar_tensor_tensor(cf(3), gk(3), 0.5, cs[:, 0, :], ALU.mult, ALU.add),
        lambda e: e.tensor_copy(cf(4), gk(2)),
        lambda e: e.tensor_tensor(cs[:, 0, :], gk(3), gk(1), ALU.subtract),
        lambda e: e.tensor_scalar(cf(5), cs[:, 0, :], 0.5, None, ALU.mult),
        lambda e: e.scalar_tensor_tensor(cs[:, 0, :], gk(2), -2.5, gk(1), ALU.mult, ALU.add),
        lambda e: e.scalar_tensor_tensor(cs[:, 0, :], gk(3), 2.0, cs[:, 0, :], ALU.mult, ALU.add),
        lambda e: e.scalar_tensor_tensor(cf(6), gk(4), -0.5, cs[:, 0, :], ALU.mult, ALU.add),
        lambda e: e.tensor_scalar(cs[:, 0, :], gk(1), -0.5, None, ALU.mult),
        lambda e: e.scalar_tensor_tensor(cs[:, 0, :], gk(2), 1.5, cs[:, 0, :], ALU.mult, ALU.add),
        lambda e: e.scalar_tensor_tensor(cs[:, 0, :], gk(3), -1.5, cs[:, 0, :], ALU.mult, ALU.add),
        lambda e: e.scalar_tensor_tensor(cf(7), gk(4), 0.5, cs[:, 0, :], ALU.mult, ALU.add),
    ]
    steps += [
        (lambda e, i=i: e.tensor_tensor(cf(4 + i), cf(4 + i), cf(i), ALU.subtract))
        for i in range(4)
    ]
    for st in steps[:-1]:
        p.op("vector", st)
    p.op("vector", steps[-1], inc=dve)

    mt, ft, cpt, ht = T["mt"], T["ft"], T["cpt"], T["ht"]
    wtile, stag = T["wtile"], T["stag"]
    tr_ps = T["tr_ps"]
    GROUP = 16 if kind != "down" else 64
    tr_free = state.setdefault("tr_free", [0, 0, 0, 0])
    wt_free = state.setdefault("wt_free", [0, 0])
    stag_free = state.setdefault("stag_free", [0, 0])
    ga = state.pop("tr_guard_act", 0)
    if ga:
        p.wait("tensor", p.ch("act"), ga)
    last_stg = 0
    for t in range(WTILES):
        gidx, tloc = divmod(t, GROUP)
        wsl = t % 2
        tsl = t % 4
        p.op("vector", lambda e, t=t: e.tensor_scalar(mt[:, :], T["iota_c"][:, :], cb_col[:, t:t + 1], None, ALU.is_ge))
        p.op("vector", lambda e, t=t: e.scalar_tensor_tensor(ft[:, :], T["cdelta"][:, :], f0_col[:, t:t + 1], mt[:, :], ALU.add, ALU.subtract))
        for i in range(4):
            p.op("vector", lambda e, t=t, i=i: e.tensor_scalar(cpt[:, i, :], mt[:, :], coef[:, 4 + i, t:t + 1], coef[:, i, t:t + 1], ALU.mult, ALU.add))
        p.op("vector", lambda e: e.tensor_tensor(ht[:, 0, :], cpt[:, 3, :], ft[:, :], ALU.mult))
        p.op("vector", lambda e: e.tensor_tensor(ht[:, 1, :], ht[:, 0, :], cpt[:, 2, :], ALU.add))
        p.op("vector", lambda e: e.tensor_tensor(ht[:, 0, :], ht[:, 1, :], ft[:, :], ALU.mult))
        p.op("vector", lambda e: e.tensor_tensor(ht[:, 1, :], ht[:, 0, :], cpt[:, 1, :], ALU.add))
        p.op("vector", lambda e: e.tensor_tensor(ht[:, 0, :], ht[:, 1, :], ft[:, :], ALU.mult))
        if wt_free[wsl]:
            p.wait("vector", pe, wt_free[wsl])
        p.op("vector", lambda e, wsl=wsl: e.tensor_tensor(wtile[:, wsl, :], ht[:, 0, :], cpt[:, 0, :], ALU.add), inc=dve)
        w_done = dve.n

        p.wait("tensor", dve, w_done)
        if tr_free[tsl]:
            p.wait("tensor", dve, tr_free[tsl])
        p.op("tensor", lambda e, wsl=wsl, tsl=tsl: e.transpose(tr_ps[:, tsl, :], wtile[:, wsl, :], T["ident_bf"][:, :]), inc=pe)
        wt_free[wsl] = pe.n
        tr_done = pe.n

        ssl = gidx % 2
        if tloc == 0 and stag_free[ssl]:
            p.wait("vector", stag_ch[ssl], stag_free[ssl])
        p.wait("vector", pe, tr_done)
        if kind != "down":
            src = tr_ps[:, tsl, :].rearrange("p (io hb) -> p hb io", hb=16)
            dst = stag[:, ssl, 0:2048].rearrange("p (hb i) -> p hb i", hb=16)[:, :, tloc * 8:tloc * 8 + 8]
        else:
            src = tr_ps[:, tsl, :].rearrange("p (q s) -> p s q", q=2)
            dst = stag[:, ssl, :].rearrange("p (s h) -> p s h", s=64)[:, :, tloc * 2:tloc * 2 + 2]
        p.op("vector", lambda e, src=src, dst=dst: e.tensor_copy(dst, src), inc=dve)
        tr_free[tsl] = dve.n
        last_stg = dve.n
        if tloc == GROUP - 1:
            p.wait("sync", dve, last_stg)
            if kind != "down":
                outap = nout_ap.rearrange("(hb c) i -> c hb i", c=128)[:, :, gidx * 128:(gidx + 1) * 128]
                srcap = stag[:, ssl, 0:2048].rearrange("p (a b) -> p a b", a=16)
            else:
                outap = nout_ap.rearrange("(s c) h -> c s h", c=128)[:, :, gidx * 128:(gidx + 1) * 128]
                srcap = stag[:, ssl, :].rearrange("p (a b) -> p a b", a=64)
            p.dma("sync", stag_ch[ssl], outap, srcap)
            stag_free[ssl] = stag_ch[ssl].n
    return last_stg



def build_L1():
    nc = bass.Bass("TRN2", target_bir_lowering=False, num_devices=NCORE)

    xT = nc.dram_tensor("xT", [D, TOK], F32, kind="ExternalInput").ap()
    wk_l = nc.dram_tensor("wk_l", [D, D], F32, kind="ExternalInput").ap()
    wq_l = nc.dram_tensor("wq_l", [D, D], F32, kind="ExternalInput").ap()
    wv_l = nc.dram_tensor("wv_l", [D, D], F32, kind="ExternalInput").ap()
    gwin_i = nc.dram_tensor("gwin", [3, 128, WTILES * 5], F32, kind="ExternalInput").ap()
    f0cb_i = nc.dram_tensor("f0cb", [3, 2, 128, WTILES], F32, kind="ExternalInput").ap()
    cdelta_i = nc.dram_tensor("cdelta", [128, 128], F32, kind="ExternalInput").ap()
    iota_i = nc.dram_tensor("iota", [128, 128], F32, kind="ExternalInput").ap()
    ident_i = nc.dram_tensor("ident", [128, 128], F32, kind="ExternalInput").ap()
    ones_i = nc.dram_tensor("ones_row", [1, 128], F32, kind="ExternalInput").ap()
    invd_i = nc.dram_tensor("invd_col_i", [128, 1], F32, kind="ExternalInput").ap()
    consts_i = nc.dram_tensor("consts_i", [128, 8], F32, kind="ExternalInput").ap()

    w_gate = nc.dram_tensor("w_gate", [D, GATE_IB], BF16, kind="ExternalOutput").ap()
    w_up = nc.dram_tensor("w_up", [D, GATE_IB], BF16, kind="ExternalOutput").ap()
    w_down = nc.dram_tensor("w_down", [INTER, DOWN_HB], BF16, kind="ExternalOutput").ap()
    k_pub = nc.dram_tensor("k_pub", [H_A, HD, TOK], BF16, kind="ExternalOutput").ap()
    v_pub = nc.dram_tensor("v_pub", [H_A, TOK, HD], BF16, kind="ExternalOutput").ap()
    q_pub = nc.dram_tensor("q_pub", [H_A, HD, TOK], BF16, kind="ExternalOutput").ap()

    with ExitStack() as ctx:
        e = ctx.enter_context
        ln_all = e(nc.sbuf_tensor("ln_all", [128, DT_PER, TOK], BF16))
        xblk = e(nc.sbuf_tensor("xblk", [128, DT_PER, TB], F32))
        xcast = e(nc.sbuf_tensor("xcast", [128, DT_PER, TB], BF16))
        xsq = e(nc.sbuf_tensor("xsq", [128, DT_PER, TB], BF16))
        rows4 = e(nc.sbuf_tensor("rows4", [1, 4, TB], F32))
        ln_t = e(nc.sbuf_tensor("ln_t", [128, TB], F32))
        invd_col = e(nc.sbuf_tensor("invd_col", [128, 1], BF16))
        const_sb = e(nc.sbuf_tensor("const_sb", [128, 8], F32))
        ones_f32 = e(nc.sbuf_tensor("ones_f32", [1, 128], F32))
        ident_bf = e(nc.sbuf_tensor("ident_bf", [128, 128], BF16))
        cdelta = e(nc.sbuf_tensor("cdelta_s", [128, 128], F32))
        iota_c = e(nc.sbuf_tensor("iota_s", [128, 128], F32))
        f0cb_s = e(nc.sbuf_tensor("f0cb_s", [128, 3, 2, WTILES], F32))
        g_all = e(nc.sbuf_tensor("g_all", [128, 3, 5 * WTILES], F32))
        coef = e(nc.sbuf_tensor("coef", [128, 8, WTILES], F32))
        cscr = e(nc.sbuf_tensor("cscr", [128, 2, WTILES], F32))
        mt = e(nc.sbuf_tensor("mt", [128, 128], F32))
        ft = e(nc.sbuf_tensor("ft", [128, 128], F32))
        cpt = e(nc.sbuf_tensor("cpt", [128, 4, 128], F32))
        ht = e(nc.sbuf_tensor("ht", [128, 2, 128], F32))
        wtile = e(nc.sbuf_tensor("wtile", [128, 2, 128], BF16))
        stag = e(nc.sbuf_tensor("stag", [128, 2, 8192], BF16))
        wslab = e(nc.sbuf_tensor("wslab", [128, 2, DT_PER, 128], BF16))
        kqv_ev = e(nc.sbuf_tensor("kqv_ev", [128, 2, TB], BF16))
        vt_ev = e(nc.sbuf_tensor("vt_ev", [128, 2, 4, 128], BF16))

        psA = e(nc.psum_tensor("psA", [128, 2 * TB], F32))
        bcA = e(nc.psum_tensor("bcA", [128, 2, TB], F32))
        proj_ps = e(nc.psum_tensor("proj_ps", [128, 2, TB], F32))
        tr_ps = e(nc.psum_tensor("tr_ps", [128, 4, 128], BF16))

        block = e(nc.Block())
        p = Prog(nc, ctx)
        pe, dve, act = p.ch("pe"), p.ch("dve"), p.ch("act")
        gc = p.ch("gconst")
        sc = p.ch("sconst")
        chx = p.ch("chx")
        slab_ch = [p.ch("slab0"), p.ch("slab1")]
        stag_ch = [p.ch("stag0"), p.ch("stag1")]
        kqv_ch = [p.ch("kqv0"), p.ch("kqv1")]
        vt_ch = [p.ch("vt0"), p.ch("vt1")]

        C = dict(xcast=xcast, xsq=xsq, psA=psA, bcA=bcA, rows4=rows4, ln_t=ln_t,
                 invd_col=invd_col[:, :], ones_f32=ones_f32[:, :],
                 eps1=const_sb[0:1, 1:2], zero1=const_sb[0:1, 0:1])
        T = dict(g_all=g_all, coef=coef, cscr=cscr, f0cb_s=f0cb_s,
                 mt=mt, ft=ft, cpt=cpt, ht=ht, wtile=wtile, stag=stag,
                 tr_ps=tr_ps, iota_c=iota_c, cdelta=cdelta, ident_bf=ident_bf)

        p.dma("gpsimd", gc, invd_col[:, :], invd_i)
        p.dma("sync", sc, const_sb[:, :], consts_i)
        p.dma("gpsimd", gc, ident_bf[:, :], ident_i)
        p.dma("sync", sc, ones_f32[:, :], ones_i)
        p.dma("sync", sc, cdelta[:, :], cdelta_i)
        p.dma("sync", sc, iota_c[:, :], iota_i)
        for w in range(3):
            p.dma("sync", sc, f0cb_s[:, w, 0, :], f0cb_i[w, 0, :, :])
            p.dma("sync", sc, f0cb_s[:, w, 1, :], f0cb_i[w, 1, :, :])
            p.dma("sync", sc, g_all[:, w, :], gwin_i[w, :, :])
        const_sc = sc.n
        const_gc = gc.n

        p.wait("vector", sc, const_sc)
        p.wait("tensor", gc, const_gc)
        p.wait("tensor", sc, const_sc)
        p.wait("scalar", sc, const_sc)

        xT_t = xT.rearrange("(dt p) t -> p dt t", p=128)
        for tb in range(NTB):
            p.dma("sync", chx, xblk[:, :, :], xT_t[:, :, tb * TB:(tb + 1) * TB])
            p.wait("vector", chx, chx.n)
            d = emit_layernorm(p, xblk, ln_all[:, :, tb * TB:(tb + 1) * TB], C)
            p.wait("sync", dve, d)
        ln_all_done = dve.n

        slab_free = [0, 0]
        ev_free = [0, 0]
        kqv_free_ch = [None, None]
        vt_free = [0, 0]
        proj_free = [0, 0]
        wsrc = {0: wk_l, 1: wq_l, 2: wv_l}
        wdst = {0: k_pub, 1: q_pub, 2: v_pub}
        for wt in range(3):
            wsrc_t = wsrc[wt].rearrange("(dt p) m -> p dt m", p=128)
            for h in range(H_A):
                sl = (wt * H_A + h) % 2
                if slab_free[sl]:
                    p.wait("gpsimd", pe, slab_free[sl])
                p.dma("gpsimd", slab_ch[sl], wslab[:, sl, :, :], wsrc_t[:, :, h * 128:(h + 1) * 128])
                p.wait("tensor", slab_ch[sl], slab_ch[sl].n)
                p.wait("tensor", dve, ln_all_done)
                for tb in range(NTB):
                    psl = tb % 2
                    if proj_free[psl]:
                        p.wait("tensor", act, proj_free[psl])
                    for dt in range(DT_PER):
                        p.op("tensor", lambda e, sl=sl, psl=psl, dt=dt, tb=tb: e.matmul(
                            proj_ps[:, psl, :], wslab[:, sl, dt, :],
                            ln_all[:, dt, tb * TB:(tb + 1) * TB],
                            start=(dt == 0), stop=(dt == DT_PER - 1)),
                            inc=(pe if dt == DT_PER - 1 else None))
                    mm_done = pe.n
                    esl = tb % 2
                    if ev_free[esl]:
                        p.wait("scalar", kqv_free_ch[esl], ev_free[esl])
                    p.wait("scalar", pe, mm_done)
                    p.op("scalar", lambda e, esl=esl, psl=psl: e.copy(kqv_ev[:, esl, :], proj_ps[:, psl, :]), inc=act)
                    ev_done = act.n
                    proj_free[psl] = ev_done
                    if wt != 2:
                        p.wait("sync", act, ev_done)
                        p.dma("sync", kqv_ch[esl], wdst[wt][h, :, tb * TB:(tb + 1) * TB], kqv_ev[:, esl, :])
                        ev_free[esl] = kqv_ch[esl].n
                        kqv_free_ch[esl] = kqv_ch[esl]
                    else:
                        p.wait("tensor", act, ev_done)
                        if vt_free[esl]:
                            p.wait("tensor", act, vt_free[esl])
                        for qq in range(4):
                            p.op("tensor", lambda e, esl=esl, qq=qq: e.transpose(
                                tr_ps[:, qq, :], kqv_ev[:, esl, qq * 128:(qq + 1) * 128],
                                ident_bf[:, :]), inc=(pe if qq == 3 else None))
                        trs_done = pe.n
                        p.wait("scalar", pe, trs_done)
                        p.op("scalar", lambda e, esl=esl: e.copy(
                            vt_ev[:, esl, :, :], tr_ps[:, :, :]), inc=act)
                        vt_done = act.n
                        vt_free[esl] = vt_done
                        p.wait("sync", act, vt_done)
                        p.dma("sync", vt_ch[esl],
                              v_pub[h, tb * TB:(tb + 1) * TB, :].rearrange("(q t) d -> t q d", t=128),
                              vt_ev[:, esl, :, :])
                        ev_free[esl] = vt_ch[esl].n
                        kqv_free_ch[esl] = vt_ch[esl]
                slab_free[sl] = pe.n

        recon_state = {"tr_guard_act": act.n}
        for w, (out_ap, kind) in enumerate(
                ((w_gate, "gate"), (w_up, "up"), (w_down, "down"))):
            recon_stg = emit_recon_weight(p, T, w, out_ap, kind, stag_ch, recon_state)

        for chn in (gc, sc, chx, *slab_ch, *stag_ch, *kqv_ch, *vt_ch):
            if chn.n:
                p.wait("gpsimd", chn, chn.n)

        p.finalize(block)
    return nc



_build_cache = {}


def _f32(x):
    return np.ascontiguousarray(np.asarray(x), dtype=np.float32)


def run_L1(inputs):
    x0 = _f32(inputs["L0_attn_a_x"])
    W_K = _f32(inputs["L0_attn_a_W_K"])
    W_Q = _f32(inputs["L0_attn_a_W_Q"])
    W_V = _f32(inputs["L0_attn_a_W_V"])
    cps = [_f32(inputs["L0_ffn_gate_cp"]), _f32(inputs["L0_ffn_up_cp"]),
           _f32(inputs["L0_ffn_down_cp"])]
    wk_l = np.ascontiguousarray(W_K.transpose(2, 0, 1).reshape(D, D))
    wq_l = np.ascontiguousarray(W_Q.transpose(2, 0, 1).reshape(D, D))
    wv_l = np.ascontiguousarray(W_V.transpose(2, 0, 1).reshape(D, D))

    if "L1" not in _build_cache:
        _build_cache["L1"] = build_L1()
    nc1 = _build_cache["L1"]

    in_maps = []
    for c in range(NCORE):
        hc = host_constants(c)
        pos = core_positions(c)
        b = c // 4
        in_maps.append(dict(
            xT=np.ascontiguousarray(x0[b].T[:, pos]),
            wk_l=wk_l, wq_l=wq_l, wv_l=wv_l,
            gwin=host_gwin(c, cps),
            f0cb=hc["f0cb"], cdelta=hc["cdelta"], iota=hc["iota"],
            ident=hc["ident"], ones_row=hc["ones_row"], invd_col_i=hc["invd_col"],
            consts_i=host_consts_vals(),
        ))
    return run_bass_kernel_spmd(nc1, in_maps, core_ids=list(range(NCORE)))


def kernel(**inputs):
    res1 = run_L1(inputs)
    res2 = run_L2(inputs, res1)
    res3 = run_L3(inputs, res2)
    out = np.empty((B, S, D), np.float32)
    for c in range(NCORE):
        b = c // 4
        pos = core_positions(c)
        out[b, pos, :] = res3.results[c]["outT"].T
    return out



def attn_unit_table(core):
    cA, cB = core_chunks(core)
    wA = np.zeros(9, np.float32)
    wB = np.zeros(9, np.float32)
    wA[0] = 1.0
    wB[1] = 1.0
    sel = [("A", j) for j in range(cA)] + [("B", j) for j in range(cB)]
    assert len(sel) == 7
    for u, (qs, j) in enumerate(sel, start=2):
        (wA if qs == "A" else wB)[u] = 1.0
    return wA, wB, sel


def host_kv_slots(core, pubK, pubV, nheads, hd=HD):
    _, _, sel = attn_unit_table(core)
    b = core // 4
    kk = np.empty((nheads, 7, hd, CHUNK), dtype=pubK[0].dtype)
    vv = np.empty((nheads, 7, CHUNK, hd), dtype=pubV[0].dtype)
    for u, (qs, j) in enumerate(sel):
        og = j if j < 4 else 7 - j
        owner = b * 4 + og
        span = slice(0, CHUNK) if j < 4 else slice(CHUNK, TOK)
        kk[:, u] = pubK[owner][:, :, span]
        vv[:, u] = pubV[owner][:, span, :]
    return kk, vv


def emit_attn_core(p, A, nheads, exp_scale):
    pe, dve, act = p.ch("pe"), p.ch("dve"), p.ch("act")
    qa_sb, kk_sb, vv_sb = A["qa_sb"], A["kk_sb"], A["vv_sb"]
    qu_sb, pt_sb = A["qu_sb"], A["pt_sb"]
    yA, yB, dA, dB = A["yA"], A["yB"], A["dA"], A["dB"]
    st_ps, y_ps, den_ps, rc_ps = A["st_ps"], A["y_ps"], A["den_ps"], A["rc_ps"]
    masks = A["masks"]
    z_buf = A["z_buf"]

    q_free = [0, 0]
    kv_free = [0, 0]
    st_free = [[0, 0], [0, 0]]
    sc_mm = [[0, 0], [0, 0]]
    pt_free = [0, 0]
    y_free = 0
    rc_free = 0

    A["load_q"](0, 0)
    A["load_kv"](0, 0, 0)
    for h in range(nheads):
        qsl = h % 2
        if h + 1 < nheads:
            nqsl = (h + 1) % 2
            if q_free[nqsl]:
                p.wait(A["q_eng"], pe, q_free[nqsl])
            A["load_q"](h + 1, nqsl)
        p.wait("tensor", A["q_ch"][qsl], A["q_n"][qsl])
        p.wait("vector", A["q_ch"][qsl], A["q_n"][qsl])
        for u in range(9):
            ksl = (h * 9 + u) % 2
            nh, nu = (h, u + 1) if u < 8 else (h + 1, 0)
            if nh < nheads:
                nksl = (h * 9 + u + 1) % 2
                if kv_free[nksl]:
                    p.wait(A["kv_eng"], pe, kv_free[nksl])
                A["load_kv"](nh, nu, nksl)
            p.wait("tensor", A["kv_chk"][ksl], A["kv_nk"][ksl])
            p.wait("tensor", A["kv_chv"][ksl], A["kv_nv"][ksl])
            if u == 0:
                qrhs = qa_sb[:, qsl, 0:CHUNK]
            elif u == 1:
                qrhs = qa_sb[:, qsl, CHUNK:TOK]
            else:
                p.op("vector", lambda e, qsl=qsl, u=u: e.tensor_scalar(
                    qu_sb[:, :], qa_sb[:, qsl, 0:CHUNK], A["uwA"][:, u:u + 1], None, ALU.mult))
                p.op("vector", lambda e, qsl=qsl, u=u: e.scalar_tensor_tensor(
                    qu_sb[:, :], qa_sb[:, qsl, CHUNK:TOK], A["uwB"][:, u:u + 1],
                    qu_sb[:, :], ALU.mult, ALU.add), inc=dve)
                p.wait("tensor", dve, dve.n)
                qrhs = qu_sb[:, :]
            diag = u < 2
            bias_col = A["bias_col"](h)

            def emit_scores(s):
                stsl = s % 2
                for h2 in range(2):
                    if st_free[stsl][h2]:
                        p.wait("tensor", act, st_free[stsl][h2])
                    p.op("tensor", lambda e, s=s, stsl=stsl, qrhs=qrhs, h2=h2, ksl=ksl: e.matmul(
                        st_ps[:, stsl, h2, :], kk_sb[:, ksl, s * 128:(s + 1) * 128],
                        qrhs[:, h2 * 512:(h2 + 1) * 512],
                        start=True, stop=True), inc=pe)
                    sc_mm[stsl][h2] = pe.n

            emit_scores(0)
            for s in range(8):
                stsl = s % 2
                psl = s % 2
                if s + 1 < 8:
                    emit_scores(s + 1)
                for h2 in range(2):
                    p.wait("scalar", pe, sc_mm[stsl][h2])
                    if pt_free[psl]:
                        p.wait("scalar", pe, pt_free[psl])
                    p.op("scalar", lambda e, h2=h2, psl=psl, stsl=stsl, bias_col=bias_col: e.activation(
                        pt_sb[:, psl, h2 * 512:(h2 + 1) * 512],
                        st_ps[:, stsl, h2, :], AF.Exp,
                        bias=bias_col, scale=exp_scale), inc=act)
                    st_free[stsl][h2] = act.n
                exp_done = act.n
                if diag:
                    p.wait("vector", act, exp_done)
                    if pt_free[psl]:
                        p.wait("vector", pe, pt_free[psl])
                    p.op("vector", lambda e, s=s, psl=psl: e.tensor_tensor(
                        pt_sb[:, psl, 0:512], pt_sb[:, psl, 0:512],
                        masks[:, s, 0, :], ALU.mult))
                    p.op("vector", lambda e, s=s, psl=psl: e.tensor_tensor(
                        pt_sb[:, psl, 512:1024], pt_sb[:, psl, 512:1024],
                        masks[:, s, 1, :], ALU.mult), inc=dve)
                    p.wait("tensor", dve, dve.n)
                else:
                    p.wait("tensor", act, exp_done)
                if s == 0 and y_free:
                    p.wait("tensor", dve, y_free)
                vtile = vv_sb[:, ksl, s, :]
                for h2 in range(2):
                    p.op("tensor", lambda e, vtile=vtile, h2=h2, s=s, psl=psl: e.matmul(
                        y_ps[:, h2, :], vtile, pt_sb[:, psl, h2 * 512:(h2 + 1) * 512],
                        start=(s == 0), stop=(s == 7)),
                        inc=(pe if (s == 7 and h2 == 1) else None))
                    p.op("tensor", lambda e, h2=h2, s=s, psl=psl: e.matmul(
                        den_ps[0:1, h2, :], A["onesb"], pt_sb[:, psl, h2 * 512:(h2 + 1) * 512],
                        start=(s == 0), stop=(s == 7)),
                        inc=(pe if (s == 7 and h2 == 1) else None))
                pt_free[psl] = pe.n
            pv_done = pe.n
            kv_free[ksl] = pv_done
            p.wait("vector", pe, pv_done)
            ypsf = y_ps[:, :, :].rearrange("p a b -> p (a b)")
            dpsf = den_ps[0:1, :, :].rearrange("p a b -> p (a b)")
            if u == 0:
                p.op("vector", lambda e: e.tensor_scalar(yA[:, :], ypsf, A["uwA"][:, 0:1], None, ALU.mult))
                p.op("vector", lambda e: e.tensor_scalar(dA[:, :], dpsf, A["uwA"][0:1, 0:1], None, ALU.mult), inc=dve)
            elif u == 1:
                p.op("vector", lambda e: e.tensor_scalar(yB[:, :], ypsf, A["uwB"][:, 1:2], None, ALU.mult))
                p.op("vector", lambda e: e.tensor_scalar(dB[:, :], dpsf, A["uwB"][0:1, 1:2], None, ALU.mult), inc=dve)
            else:
                p.op("vector", lambda e, u=u: e.scalar_tensor_tensor(
                    yA[:, :], ypsf, A["uwA"][:, u:u + 1], yA[:, :], ALU.mult, ALU.add))
                p.op("vector", lambda e, u=u: e.scalar_tensor_tensor(
                    dA[:, :], dpsf, A["uwA"][0:1, u:u + 1], dA[:, :], ALU.mult, ALU.add))
                p.op("vector", lambda e, u=u: e.scalar_tensor_tensor(
                    yB[:, :], ypsf, A["uwB"][:, u:u + 1], yB[:, :], ALU.mult, ALU.add))
                p.op("vector", lambda e, u=u: e.scalar_tensor_tensor(
                    dB[:, :], dpsf, A["uwB"][0:1, u:u + 1], dB[:, :], ALU.mult, ALU.add), inc=dve)
            y_free = dve.n
        q_free[qsl] = pe.n
        p.op("vector", lambda e: e.reciprocal(A["rrow"][:, 0, :], dA[:, :]))
        p.op("vector", lambda e: e.reciprocal(A["rrow"][:, 1, :], dB[:, :]), inc=dve)
        p.wait("tensor", dve, dve.n)
        if rc_free:
            p.wait("tensor", dve, rc_free)
        zi = A["zmap"](h)
        for part, y_acc in ((0, yA), (1, yB)):
            for h2 in range(2):
                p.op("tensor", lambda e, part=part, h2=h2: e.matmul(
                    rc_ps[:, h2, :], A["ones_f32"],
                    A["rrow"][:, part, h2 * 512:(h2 + 1) * 512],
                    start=True, stop=True), inc=(pe if h2 == 1 else None))
            rc_mm = pe.n
            p.wait("vector", pe, rc_mm)
            p.op("vector", lambda e, zi=zi, part=part, y_acc=y_acc: e.tensor_tensor(
                z_buf[:, zi, part * CHUNK:(part + 1) * CHUNK], y_acc[:, :],
                rc_ps[:, :, :].rearrange("p a b -> p (a b)"), ALU.mult), inc=dve)
            rc_free = dve.n
            if part == 0:
                p.wait("tensor", dve, rc_free)
        y_free = max(y_free, rc_free)
    return dve.n



def emit_ffn(p, Fd, x_src_ap, x_dst_ap, C):
    pe, dve, act = p.ch("pe"), p.ch("dve"), p.ch("act")
    gslab, dslab = Fd["gslab"], Fd["dslab"]
    at_sb = Fd["at_sb"]
    xin_sb, xout_sb = Fd["xin_sb"], Fd["xout_sb"]
    gu_ps, dn_ps = Fd["gu_ps"], Fd["dn_ps"]
    sil_sb = Fd["sil_sb"]
    ln_out = C["xcast"]

    gfree = [0, 0]
    dfree = [0, 0]
    gu_free = [0, 0]
    dn_free = [0, 0]
    xi_free = 0
    xo_free = 0
    at_free = 0

    wg_t = Fd["wg"].rearrange("(dt p) i -> p dt i", p=128)
    wu_t = Fd["wu"].rearrange("(dt p) i -> p dt i", p=128)
    wd_t = Fd["wd"].rearrange("(it p) h -> p it h", p=128)

    for tb in range(NTB):
        if xi_free:
            p.wait(Fd["xi_eng"], dve, xi_free)
        if xo_free:
            p.wait(Fd["xi_eng"], Fd["ch_xo"], xo_free)
        p.dma(Fd["xi_eng"], Fd["ch_xi"], xin_sb[:, :, :], x_src_ap[:, tb * TB:(tb + 1) * TB].rearrange("(dt p) t -> p dt t", p=128))
        p.wait("vector", Fd["ch_xi"], Fd["ch_xi"].n)
        ln_done = emit_layernorm(p, xin_sb, ln_out, C)

        if at_free:
            p.wait("vector", pe, at_free)
        for it in range(IT_PER):
            gsl = it % 2
            if gfree[gsl]:
                p.wait(Fd["g_eng"], pe, gfree[gsl])
            p.dma(Fd["g_eng"], Fd["ch_g"][gsl], gslab[:, gsl, 0, :, :], wg_t[:, :, it * 128:(it + 1) * 128])
            p.dma(Fd["g_eng"], Fd["ch_g"][gsl], gslab[:, gsl, 1, :, :], wu_t[:, :, it * 128:(it + 1) * 128])
            gn = Fd["ch_g"][gsl].n
            p.wait("tensor", Fd["ch_g"][gsl], gn)
            p.wait("tensor", dve, ln_done)
            for j in range(2):
                if gu_free[j]:
                    p.wait("tensor", act if j == 0 else dve, gu_free[j])
                for dt in range(DT_PER):
                    p.op("tensor", lambda e, gsl=gsl, j=j, dt=dt: e.matmul(
                        gu_ps[:, j, :], gslab[:, gsl, j, dt, :], ln_out[:, dt, :],
                        start=(dt == 0), stop=(dt == DT_PER - 1)),
                        inc=(pe if dt == DT_PER - 1 else None))
            mm_done = pe.n
            gfree[gsl] = mm_done
            p.wait("scalar", pe, mm_done)
            p.op("scalar", lambda e: e.activation(sil_sb[:, :], gu_ps[:, 0, :], AF.Silu,
                                                  bias=C["zero128"], scale=1.0), inc=act)
            gu_free[0] = act.n
            p.wait("vector", act, act.n)
            p.op("vector", lambda e, it=it: e.tensor_tensor(
                at_sb[:, it, :], sil_sb[:, :], gu_ps[:, 1, :], ALU.mult), inc=dve)
            gu_free[1] = dve.n
        a_done = dve.n

        p.wait("tensor", dve, a_done)
        if xo_free:
            p.wait("vector", Fd["ch_xo"], xo_free)
        for dt in range(DT_PER):
            dsl = dt % 2
            if dfree[dsl]:
                p.wait(Fd["d_eng"], pe, dfree[dsl])
            p.dma(Fd["d_eng"], Fd["ch_d"][dsl], dslab[:, dsl, :, :], wd_t[:, :, dt * 128:(dt + 1) * 128])
            dn_ = Fd["ch_d"][dsl].n
            p.wait("tensor", Fd["ch_d"][dsl], dn_)
            psl = dt % 2
            if dn_free[psl]:
                p.wait("tensor", dve, dn_free[psl])
            for it in range(IT_PER):
                p.op("tensor", lambda e, dsl=dsl, psl=psl, it=it: e.matmul(
                    dn_ps[:, psl, :], dslab[:, dsl, it, :], at_sb[:, it, :],
                    start=(it == 0), stop=(it == IT_PER - 1)),
                    inc=(pe if it == IT_PER - 1 else None))
            dmm = pe.n
            dfree[dsl] = dmm
            p.wait("vector", pe, dmm)
            p.op("vector", lambda e, dt=dt, psl=psl: e.tensor_tensor(
                xout_sb[:, dt, :], xin_sb[:, dt, :], dn_ps[:, psl, :], ALU.add), inc=dve)
            dn_free[psl] = dve.n
        at_free = pe.n
        xi_free = dve.n
        add_done = dve.n
        p.wait(Fd["xo_eng"], dve, add_done)
        p.dma(Fd["xo_eng"], Fd["ch_xo"], x_dst_ap[:, tb * TB:(tb + 1) * TB].rearrange("(dt p) t -> p dt t", p=128), xout_sb[:, :, :])
        xo_free = Fd["ch_xo"].n
    return Fd["ch_xo"].n


def _barrier(p):
    for eng in ENGINES:
        for ch in list(p.chs.values()):
            if ch.n:
                p.wait(eng, ch, ch.n)


def build_L2():
    nc = bass.Bass("TRN2", target_bir_lowering=False, num_devices=NCORE)

    xT = nc.dram_tensor("xT", [D, TOK], F32, kind="ExternalInput").ap()
    qa_i = nc.dram_tensor("qa_i", [H_A, HD, TOK], BF16, kind="ExternalInput").ap()
    ka_own = nc.dram_tensor("ka_own", [H_A, HD, TOK], BF16, kind="ExternalInput").ap()
    va_own = nc.dram_tensor("va_own", [H_A, TOK, HD], BF16, kind="ExternalInput").ap()
    ka_sl = nc.dram_tensor("ka_sl", [H_A, 7, HD, CHUNK], BF16, kind="ExternalInput").ap()
    va_sl = nc.dram_tensor("va_sl", [H_A, 7, CHUNK, HD], BF16, kind="ExternalInput").ap()
    uw_i = nc.dram_tensor("uw_i", [2, 128, 9], F32, kind="ExternalInput").ap()
    masks_i = nc.dram_tensor("masks_i", [128, 8, 2, 512], F32, kind="ExternalInput").ap()
    wo_l = nc.dram_tensor("wo_l", [D, D], F32, kind="ExternalInput").ap()
    wg_i = nc.dram_tensor("wg_i", [D, INTER], BF16, kind="ExternalInput").ap()
    wu_i = nc.dram_tensor("wu_i", [D, INTER], BF16, kind="ExternalInput").ap()
    wd_i = nc.dram_tensor("wd_i", [INTER, D], BF16, kind="ExternalInput").ap()
    wqb_i = nc.dram_tensor("wqb_i", [D, D], F32, kind="ExternalInput").ap()
    wkb_i = nc.dram_tensor("wkb_i", [D, KVD], F32, kind="ExternalInput").ap()
    wvb_i = nc.dram_tensor("wvb_i", [D, KVD], F32, kind="ExternalInput").ap()
    qA_i = nc.dram_tensor("qA_i", [D, RANK], F32, kind="ExternalInput").ap()
    qB_i = nc.dram_tensor("qB_i", [RANK, D], F32, kind="ExternalInput").ap()
    kA_i = nc.dram_tensor("kA_i", [D, RANK], F32, kind="ExternalInput").ap()
    kB_i = nc.dram_tensor("kB_i", [RANK, KVD], F32, kind="ExternalInput").ap()
    vA_i = nc.dram_tensor("vA_i", [D, RANK], F32, kind="ExternalInput").ap()
    vB_i = nc.dram_tensor("vB_i", [RANK, KVD], F32, kind="ExternalInput").ap()
    qgain_i = nc.dram_tensor("qgain_i", [1, H_A], F32, kind="ExternalInput").ap()
    cos_i = nc.dram_tensor("cos_i", [128, TOK], F32, kind="ExternalInput").ap()
    sin_i = nc.dram_tensor("sin_i", [128, TOK], F32, kind="ExternalInput").ap()
    jt_i = nc.dram_tensor("jt_i", [128, 128], F32, kind="ExternalInput").ap()
    ident_i = nc.dram_tensor("ident", [128, 128], F32, kind="ExternalInput").ap()
    ones_i = nc.dram_tensor("ones_row", [1, 128], F32, kind="ExternalInput").ap()
    invd_i = nc.dram_tensor("invd_col_i", [128, 1], F32, kind="ExternalInput").ap()
    invh_i = nc.dram_tensor("invh_col_i", [128, 1], F32, kind="ExternalInput").ap()
    onesc_i = nc.dram_tensor("onesc_i", [128, 1], F32, kind="ExternalInput").ap()
    consts_i = nc.dram_tensor("consts_i", [128, 8], F32, kind="ExternalInput").ap()

    x2T = nc.dram_tensor("x2T", [D, TOK], F32, kind="ExternalOutput").ap()
    kb_pub = nc.dram_tensor("kb_pub", [HKV, HD, TOK], BF16, kind="ExternalOutput").ap()
    vb_pub = nc.dram_tensor("vb_pub", [HKV, TOK, HD], BF16, kind="ExternalOutput").ap()
    vbT_own = nc.dram_tensor("vbT_own", [HKV, HD, TOK], BF16, kind="ExternalOutput").ap()
    qb_pub = nc.dram_tensor("qb_pub", [H_A, HD, TOK], BF16, kind="ExternalOutput").ap()
    resid_d = nc.dram_tensor("resid_d", [D, TOK], F32)

    with ExitStack() as ctx:
        e = ctx.enter_context
        xcast = e(nc.sbuf_tensor("xcast", [128, DT_PER, TB], BF16))
        xsq = e(nc.sbuf_tensor("xsq", [128, DT_PER, TB], BF16))
        rows4 = e(nc.sbuf_tensor("rows4", [1, 4, TB], F32))
        ln_t = e(nc.sbuf_tensor("ln_t", [128, TB], F32))
        invd_col = e(nc.sbuf_tensor("invd_col", [128, 1], BF16))
        invh_col = e(nc.sbuf_tensor("invh_col", [128, 1], BF16))
        onesb_col = e(nc.sbuf_tensor("onesb_col", [128, 1], BF16))
        ones_f32 = e(nc.sbuf_tensor("ones_f32", [1, 128], F32))
        const_sb = e(nc.sbuf_tensor("const_sb", [128, 8], F32))
        uw_sb = e(nc.sbuf_tensor("uw_sb", [128, 2, 9], F32))
        masks_sb = e(nc.sbuf_tensor("masks_sb", [128, 8, 2, 512], BF16))
        qg_sb = e(nc.sbuf_tensor("qg_sb", [1, H_A], F32))

        block = e(nc.Block())
        p = Prog(nc, ctx)
        pe, dve, act = p.ch("pe"), p.ch("dve"), p.ch("act")
        gc, sc = p.ch("gconst"), p.ch("sconst")

        C = dict(xcast=xcast, xsq=xsq, rows4=rows4, ln_t=ln_t,
                 invd_col=invd_col[:, :], ones_f32=ones_f32[:, :],
                 eps1=const_sb[0:1, 1:2], zero1=const_sb[0:1, 0:1],
                 zero128=const_sb[:, 0:1],
                 epsr1=const_sb[0:1, 2:3], shiftA=const_sb[:, 3:4])

        p.dma("gpsimd", gc, invd_col[:, :], invd_i)
        p.dma("gpsimd", gc, invh_col[:, :], invh_i)
        p.dma("gpsimd", gc, onesb_col[:, :], onesc_i)
        p.dma("gpsimd", gc, masks_sb[:, :, :, :], masks_i)
        p.dma("sync", sc, ones_f32[:, :], ones_i)
        p.dma("sync", sc, const_sb[:, :], consts_i)
        p.dma("sync", sc, uw_sb[:, 0, :], uw_i[0, :, :])
        p.dma("sync", sc, uw_sb[:, 1, :], uw_i[1, :, :])
        p.dma("sync", sc, qg_sb[:, :], qgain_i)
        for eng in ("tensor", "vector", "scalar"):
            p.wait(eng, gc, gc.n)
            p.wait(eng, sc, sc.n)

        with ExitStack() as s1:
            e1 = s1.enter_context
            z_buf = e1(nc.sbuf_tensor("z_buf", [128, H_A, TOK], BF16))
            qa_sb = e1(nc.sbuf_tensor("qa_sb", [128, 2, TOK], BF16))
            kk_sb = e1(nc.sbuf_tensor("kk_sb", [128, 2, CHUNK], BF16))
            vv_sb = e1(nc.sbuf_tensor("vv_sb", [128, 2, 8, 128], BF16))
            qu_sb = e1(nc.sbuf_tensor("qu_sb", [128, CHUNK], BF16))
            pt_sb = e1(nc.sbuf_tensor("pt_sb", [128, 2, CHUNK], BF16))
            yA = e1(nc.sbuf_tensor("yA", [128, CHUNK], F32))
            yB = e1(nc.sbuf_tensor("yB", [128, CHUNK], F32))
            dAr = e1(nc.sbuf_tensor("dAr", [1, CHUNK], F32))
            dBr = e1(nc.sbuf_tensor("dBr", [1, CHUNK], F32))
            rrow = e1(nc.sbuf_tensor("rrow", [1, 2, CHUNK], F32))
            st_ps = e1(nc.psum_tensor("st_ps", [128, 2, 2, 512], F32))
            y_ps = e1(nc.psum_tensor("y_ps", [128, 2, 512], F32))
            dr_ps = e1(nc.psum_tensor("dr_ps", [128, 2, 512], F32))

            qch = [p.ch("qa0"), p.ch("qa1")]
            kch = [p.ch("ka0"), p.ch("ka1")]
            vch = [p.ch("va0"), p.ch("va1")]
            A = dict(qa_sb=qa_sb, kk_sb=kk_sb, vv_sb=vv_sb, qu_sb=qu_sb,
                     pt_sb=pt_sb, yA=yA, yB=yB, dA=dAr, dB=dBr, rrow=rrow,
                     st_ps=st_ps, y_ps=y_ps, den_ps=dr_ps[0:1, :, :], rc_ps=dr_ps,
                     masks=masks_sb, z_buf=z_buf,
                     uwA=uw_sb[:, 0, :], uwB=uw_sb[:, 1, :],
                     onesb=onesb_col[:, :], ones_f32=ones_f32[:, :],
                     q_ch=qch, q_n=[0, 0], kv_chk=kch, kv_chv=vch,
                     kv_nk=[0, 0], kv_nv=[0, 0],
                     q_eng="sync", kv_eng="gpsimd",
                     zmap=lambda h: h,
                     bias_col=lambda h: C["shiftA"])

            def load_q(h, slot):
                p.dma("sync", qch[slot], qa_sb[:, slot, :], qa_i[h, :, :])
                A["q_n"][slot] = qch[slot].n
            A["load_q"] = load_q

            def load_kv(h, u, slot):
                if u == 0:
                    ksrc, vsrc = ka_own[h, :, 0:CHUNK], va_own[h, 0:CHUNK, :]
                elif u == 1:
                    ksrc, vsrc = ka_own[h, :, CHUNK:TOK], va_own[h, CHUNK:TOK, :]
                else:
                    ksrc, vsrc = ka_sl[h, u - 2, :, :], va_sl[h, u - 2, :, :]
                p.dma("gpsimd", kch[slot], kk_sb[:, slot, :], ksrc)
                A["kv_nk"][slot] = kch[slot].n
                p.dma("gpsimd", vch[slot], vv_sb[:, slot, :, :],
                      vsrc.rearrange("(c p) d -> p c d", p=128))
                A["kv_nv"][slot] = vch[slot].n
            A["load_kv"] = load_kv

            z_done = emit_attn_core(p, A, H_A, 0.08838834764831845)

            woslab = e1(nc.sbuf_tensor("woslab", [128, 2, DT_PER, 128], BF16))
            xin1 = e1(nc.sbuf_tensor("xin1", [128, TB], F32))
            wo_t = wo_l.rearrange("(ft p) m -> p ft m", p=128)
            xT_t = xT.rearrange("(dt p) t -> p dt t", p=128)
            rd_t = resid_d.rearrange("(dt p) t -> p dt t", p=128)
            wo_ch = [p.ch("wo0"), p.ch("wo1")]
            x1_ch = p.ch("x1c")
            xi_ch = p.ch("xic")
            wo_free = [0, 0]
            ps_free = [0, 0]
            p.wait("tensor", dve, z_done)
            for dt in range(DT_PER):
                wsl = dt % 2
                if wo_free[wsl]:
                    p.wait("gpsimd", pe, wo_free[wsl])
                p.dma("gpsimd", wo_ch[wsl], woslab[:, wsl, :, :], wo_t[:, :, dt * 128:(dt + 1) * 128])
                p.wait("tensor", wo_ch[wsl], wo_ch[wsl].n)
                for tb in range(NTB):
                    psl = tb % 2
                    if ps_free[psl]:
                        p.wait("tensor", dve, ps_free[psl])
                    for ft in range(DT_PER):
                        p.op("tensor", lambda e, wsl=wsl, psl=psl, ft=ft, tb=tb: e.matmul(
                            st_ps[:, psl, 0, :], woslab[:, wsl, ft, :],
                            z_buf[:, ft, tb * TB:(tb + 1) * TB],
                            start=(ft == 0), stop=(ft == DT_PER - 1)),
                            inc=(pe if ft == DT_PER - 1 else None))
                    wo_mm = pe.n
                    p.wait("sync", dve, dve.n)
                    p.wait("sync", x1_ch, x1_ch.n)
                    p.dma("sync", xi_ch, xin1[:, :], xT_t[:, dt, tb * TB:(tb + 1) * TB])
                    p.wait("vector", xi_ch, xi_ch.n)
                    p.wait("vector", pe, wo_mm)
                    p.op("vector", lambda e, psl=psl: e.tensor_tensor(
                        xin1[:, :], xin1[:, :], st_ps[:, psl, 0, :], ALU.add), inc=dve)
                    ps_free[psl] = dve.n
                    p.wait("sync", dve, dve.n)
                    p.dma("sync", x1_ch, rd_t[:, dt, tb * TB:(tb + 1) * TB], xin1[:, :])
                wo_free[wsl] = pe.n

        _barrier(p)

        with ExitStack() as s2:
            e2 = s2.enter_context
            at_sb = e2(nc.sbuf_tensor("at_sb", [128, IT_PER, TB], BF16))
            gslab = e2(nc.sbuf_tensor("gslab", [128, 2, 2, DT_PER, 128], BF16))
            dslab = e2(nc.sbuf_tensor("dslab", [128, 2, IT_PER, 128], BF16))
            xin_sb = e2(nc.sbuf_tensor("xin_sb", [128, DT_PER, TB], F32))
            sil_sb = e2(nc.sbuf_tensor("sil_sb", [128, TB], BF16))
            gu_ps = e2(nc.psum_tensor("gu_ps", [128, 2, TB], F32))
            dn_ps = e2(nc.psum_tensor("dn_ps", [128, 2, TB], F32))
            psA2 = e2(nc.psum_tensor("psA2", [128, 2 * TB], F32))
            bcA2 = e2(nc.psum_tensor("bcA2", [128, 2, TB], F32))
            C["psA"] = psA2
            C["bcA"] = bcA2
            Fd = dict(wg=wg_i, wu=wu_i, wd=wd_i,
                      gslab=gslab, dslab=dslab, at_sb=at_sb,
                      xin_sb=xin_sb, xout_sb=xin_sb,
                      gu_ps=gu_ps, dn_ps=dn_ps, sil_sb=sil_sb,
                      ch_g=[p.ch("ffg0"), p.ch("ffg1")],
                      ch_d=[p.ch("ffd0"), p.ch("ffd1")],
                      ch_xi=p.ch("ffxi"), ch_xo=p.ch("ffxo"),
                      g_eng="gpsimd", d_eng="gpsimd", xi_eng="sync", xo_eng="sync")
            emit_ffn(p, Fd, resid_d, x2T, C)

        _barrier(p)

        with ExitStack() as s3:
            e3 = s3.enter_context
            cos_sb = e3(nc.sbuf_tensor("cos_sb", [128, TOK], F32))
            sin_sb = e3(nc.sbuf_tensor("sin_sb", [128, TOK], F32))
            jt_sb = e3(nc.sbuf_tensor("jt_sb", [128, 128], BF16))
            ident_bf = e3(nc.sbuf_tensor("ident_bf", [128, 128], BF16))
            bslab = e3(nc.sbuf_tensor("bslab", [128, 2, DT_PER, 128], BF16))
            adA_sb = e3(nc.sbuf_tensor("adA_sb", [128, 3, DT_PER, RANK], BF16))
            adB_sb = e3(nc.sbuf_tensor("adB_sb", [16, 3, D], BF16))
            uad_sb = e3(nc.sbuf_tensor("uad_sb", [16, 3, TB], BF16))
            xb_sb = e3(nc.sbuf_tensor("xb_sb", [128, DT_PER, TB], F32))
            qraw_sb = e3(nc.sbuf_tensor("qraw_sb", [128, TB], F32))
            sqq_sb = e3(nc.sbuf_tensor("sqq_sb", [128, TB], BF16))
            qn_sb = e3(nc.sbuf_tensor("qn_sb", [128, TB], BF16))
            r1_sb = e3(nc.sbuf_tensor("r1_sb", [128, TB], F32))
            rot_sb = e3(nc.sbuf_tensor("rot_sb", [128, 2, TB], BF16))
            vtb_sb = e3(nc.sbuf_tensor("vtb_sb", [128, 4, 128], BF16))
            mrow = e3(nc.sbuf_tensor("mrow", [1, TB], F32))
            bj_ps = e3(nc.psum_tensor("bj_ps", [128, 2, TB], F32))
            ut_ps = e3(nc.psum_tensor("ut_ps", [128, 2, TB], F32))
            tr_ps = e3(nc.psum_tensor("tr_ps", [128, 4, 128], BF16))
            C["psA"] = ut_ps[:, :, :].rearrange("p a b -> p (a b)")
            C["bcA"] = bj_ps

            bsc = p.ch("bsc")
            bscg = p.ch("bscg")
            p.dma("sync", bsc, cos_sb[:, :], cos_i)
            p.dma("sync", bsc, sin_sb[:, :], sin_i)
            p.dma("gpsimd", bscg, jt_sb[:, :], jt_i)
            p.dma("gpsimd", bscg, ident_bf[:, :], ident_i)
            p.dma("gpsimd", bscg, adA_sb[:, 0, :, :], qA_i.rearrange("(dt p) r -> p dt r", p=128))
            p.dma("gpsimd", bscg, adA_sb[:, 1, :, :], kA_i.rearrange("(dt p) r -> p dt r", p=128))
            p.dma("gpsimd", bscg, adA_sb[:, 2, :, :], vA_i.rearrange("(dt p) r -> p dt r", p=128))
            p.dma("gpsimd", bscg, adB_sb[:, 0, :], qB_i)
            p.dma("gpsimd", bscg, adB_sb[0:16, 1, 0:KVD], kB_i)
            p.dma("gpsimd", bscg, adB_sb[0:16, 2, 0:KVD], vB_i)
            for eng in ("tensor", "vector", "scalar"):
                p.wait(eng, bsc, bsc.n)
                p.wait(eng, bscg, bscg.n)

            x2_t = x2T.rearrange("(dt p) t -> p dt t", p=128)
            bch = [p.ch("bsl0"), p.ch("bsl1")]
            xbch = p.ch("xbc")
            pub_ch = p.ch("pubb")
            bfree = [0, 0]
            xb_free = 0
            PROJ = [("q", wqb_i, 16, True), ("k", wkb_i, 4, False), ("v", wvb_i, 4, False)]
            for tb in range(NTB):
                if xb_free:
                    p.wait("sync", dve, xb_free)
                p.dma("sync", xbch, xb_sb[:, :, :], x2_t[:, :, tb * TB:(tb + 1) * TB])
                p.wait("vector", xbch, xbch.n)
                ln_done = emit_layernorm(p, xb_sb, C["xcast"], C)
                xb_free = ln_done
                p.wait("tensor", dve, ln_done)
                for ai in range(3):
                    if ai > 0:
                        p.wait("tensor", act, act.n)
                    for dt in range(DT_PER):
                        p.op("tensor", lambda e, ai=ai, dt=dt: e.matmul(
                            ut_ps[0:RANK, 0, :], adA_sb[:, ai, dt, :], C["xcast"][:, dt, :],
                            start=(dt == 0), stop=(dt == DT_PER - 1)),
                            inc=(pe if dt == DT_PER - 1 else None))
                    ad_mm = pe.n
                    p.wait("scalar", pe, ad_mm)
                    p.op("scalar", lambda e, ai=ai: e.copy(uad_sb[:, ai, :], ut_ps[0:RANK, 0, :]), inc=act)
                uad_done = act.n
                p.wait("tensor", act, uad_done)
                for name, wsrc, nch, do_gain in PROJ:
                    wt_t = wsrc.rearrange("(dt p) m -> p dt m", p=128)
                    ai = {"q": 0, "k": 1, "v": 2}[name]
                    for hc in range(nch):
                        bsl = hc % 2
                        psl = hc % 2
                        if bfree[bsl]:
                            p.wait("gpsimd", pe, bfree[bsl])
                        p.dma("gpsimd", bch[bsl], bslab[:, bsl, :, :], wt_t[:, :, hc * 128:(hc + 1) * 128])
                        p.wait("tensor", bch[bsl], bch[bsl].n)
                        p.wait("tensor", act, act.n)
                        p.wait("tensor", dve, dve.n)
                        for dt in range(DT_PER):
                            p.op("tensor", lambda e, bsl=bsl, psl=psl, dt=dt: e.matmul(
                                bj_ps[:, psl, :], bslab[:, bsl, dt, :], C["xcast"][:, dt, :],
                                start=(dt == 0), stop=False))
                        p.op("tensor", lambda e, ai=ai, hc=hc, psl=psl: e.matmul(
                            bj_ps[:, psl, :], adB_sb[:, ai, hc * 128:(hc + 1) * 128],
                            uad_sb[:, ai, :], start=False, stop=True), inc=pe)
                        raw_mm = pe.n
                        bfree[bsl] = raw_mm
                        if name == "v":
                            p.wait("scalar", pe, raw_mm)
                            p.wait("scalar", pub_ch, pub_ch.n)
                            p.op("scalar", lambda e, psl=psl: e.copy(qn_sb[:, :], bj_ps[:, psl, :]), inc=act)
                            vev = act.n
                            p.wait("sync", act, vev)
                            p.dma("sync", pub_ch, vbT_own[hc, :, tb * TB:(tb + 1) * TB], qn_sb[:, :])
                            p.wait("tensor", act, vev)
                            for qq in range(4):
                                p.op("tensor", lambda e, qq=qq: e.transpose(
                                    tr_ps[:, qq, :], qn_sb[:, qq * 128:(qq + 1) * 128],
                                    ident_bf[:, :]), inc=(pe if qq == 3 else None))
                            trd = pe.n
                            p.wait("scalar", pe, trd)
                            p.op("scalar", lambda e: e.copy(vtb_sb[:, :, :], tr_ps[:, :, :]), inc=act)
                            p.wait("sync", act, act.n)
                            p.dma("sync", pub_ch,
                                  vb_pub[hc, tb * TB:(tb + 1) * TB, :].rearrange("(q t) d -> t q d", t=128),
                                  vtb_sb[:, :, :])
                            continue
                        p.wait("scalar", pe, raw_mm)
                        p.op("scalar", lambda e, psl=psl: e.copy(qraw_sb[:, :], bj_ps[:, psl, :]), inc=act)
                        p.wait("vector", act, act.n)
                        p.op("vector", lambda e: e.tensor_tensor(sqq_sb[:, :], qraw_sb[:, :], qraw_sb[:, :], ALU.mult), inc=dve)
                        p.wait("tensor", dve, dve.n)
                        p.op("tensor", lambda e: e.matmul(ut_ps[0:1, 0, :], invh_col[:, :], sqq_sb[:, :],
                                                          start=True, stop=True), inc=pe)
                        p.wait("scalar", pe, pe.n)
                        p.op("scalar", lambda e: e.activation(mrow[:, :], ut_ps[0:1, 0, :], AF.Ln,
                                                              bias=C["epsr1"], scale=1.0), inc=act)
                        p.op("scalar", lambda e: e.activation(mrow[:, :], mrow[:, :], AF.Exp,
                                                              bias=C["zero1"], scale=-0.5), inc=act)
                        rstd_done = act.n
                        if do_gain:
                            p.wait("vector", act, rstd_done)
                            p.op("vector", lambda e, hc=hc: e.tensor_scalar(
                                mrow[:, :], mrow[:, :], qg_sb[0:1, hc:hc + 1], None, ALU.mult), inc=dve)
                            p.wait("tensor", dve, dve.n)
                        else:
                            p.wait("tensor", act, rstd_done)
                        p.op("tensor", lambda e: e.matmul(ut_ps[:, 1, :], ones_f32[:, :], mrow[:, :],
                                                          start=True, stop=True), inc=pe)
                        gb_mm = pe.n
                        p.wait("vector", pe, gb_mm)
                        p.wait("vector", pub_ch, pub_ch.n)
                        p.op("vector", lambda e: e.tensor_tensor(qn_sb[:, :], qraw_sb[:, :], ut_ps[:, 1, :], ALU.mult), inc=dve)
                        qn_done = dve.n
                        p.wait("tensor", dve, qn_done)
                        p.op("tensor", lambda e: e.matmul(ut_ps[:, 1, :], jt_sb[:, :], qn_sb[:, :],
                                                          start=True, stop=True), inc=pe)
                        jq_mm = pe.n
                        p.wait("vector", pe, jq_mm)
                        p.wait("vector", pub_ch, pub_ch.n)
                        p.op("vector", lambda e, tb=tb: e.tensor_tensor(
                            r1_sb[:, :], qn_sb[:, :], cos_sb[:, tb * TB:(tb + 1) * TB], ALU.mult))
                        p.op("vector", lambda e, tb=tb: e.tensor_tensor(
                            ln_t[:, :], ut_ps[:, 1, :], sin_sb[:, tb * TB:(tb + 1) * TB], ALU.mult))
                        rsl = hc % 2
                        p.op("vector", lambda e, rsl=rsl: e.tensor_tensor(
                            rot_sb[:, rsl, :], r1_sb[:, :], ln_t[:, :], ALU.add), inc=dve)
                        rot_done = dve.n
                        p.wait("sync", dve, rot_done)
                        dst = qb_pub if name == "q" else kb_pub
                        p.dma("sync", pub_ch, dst[hc, :, tb * TB:(tb + 1) * TB], rot_sb[:, rsl, :])

        for chn in list(p.chs.values()):
            if chn.n and chn.name not in ("pe", "dve", "act", "gp"):
                p.wait("gpsimd", chn, chn.n)
        p.finalize(block)
    return nc


def run_L2(inputs, res1):
    x0 = _f32(inputs["L0_attn_a_x"])
    if "L2" not in _build_cache:
        _build_cache["L2"] = build_L2()
    nc2 = _build_cache["L2"]

    r = res1.results
    r_l1 = r
    pubK = [r[c]["k_pub"] for c in range(NCORE)]
    pubV = [r[c]["v_pub"] for c in range(NCORE)]
    wg_full = np.concatenate([r1c["w_gate"] for r1c in r_l1], axis=1)
    wu_full = np.concatenate([r1c["w_up"] for r1c in r_l1], axis=1)
    wd_full = np.concatenate([r1c["w_down"] for r1c in r_l1], axis=1)
    _build_cache["wg_full"] = wg_full
    _build_cache["wu_full"] = wu_full
    _build_cache["wd_full"] = wd_full
    wo_l = np.ascontiguousarray(_f32(inputs["L0_attn_a_W_O"]).T)
    cvals = host_consts_vals()
    hc0 = host_constants(0)

    in_maps = []
    for c in range(NCORE):
        pos = core_positions(c)
        b = c // 4
        wA, wB, _ = attn_unit_table(c)
        kk, vv = host_kv_slots(c, pubK, pubV, H_A)
        cosT, sinT = rope_tables(pos)
        in_maps.append(dict(
            xT=np.ascontiguousarray(x0[b].T[:, pos]),
            qa_i=r[c]["q_pub"], ka_own=r[c]["k_pub"], va_own=r[c]["v_pub"],
            ka_sl=kk, va_sl=vv,
            uw_i=np.stack([np.broadcast_to(wA, (128, 9)),
                           np.broadcast_to(wB, (128, 9))]).astype(np.float32),
            masks_i=host_diag_masks(),
            wo_l=wo_l, wg_i=wg_full, wu_i=wu_full, wd_i=wd_full,
            wqb_i=_f32(inputs["L0_attn_b_Wq"]), wkb_i=_f32(inputs["L0_attn_b_Wk"]),
            wvb_i=_f32(inputs["L0_attn_b_Wv"]),
            qA_i=_f32(inputs["L0_attn_b_q_A"]), qB_i=_f32(inputs["L0_attn_b_q_B"]),
            kA_i=_f32(inputs["L0_attn_b_k_A"]), kB_i=_f32(inputs["L0_attn_b_k_B"]),
            vA_i=_f32(inputs["L0_attn_b_v_A"]), vB_i=_f32(inputs["L0_attn_b_v_B"]),
            qgain_i=_f32(inputs["L0_attn_b_q_gain"]).reshape(1, H_A),
            cos_i=cosT, sin_i=sinT, jt_i=rope_J(),
            ident=hc0["ident"], ones_row=hc0["ones_row"],
            invd_col_i=hc0["invd_col"],
            invh_col_i=np.full((128, 1), 1.0 / HD, np.float32),
            onesc_i=np.ones((128, 1), np.float32),
            consts_i=cvals,
        ))
    return run_bass_kernel_spmd(nc2, in_maps, core_ids=list(range(NCORE)))


def build_L3():
    nc = bass.Bass("TRN2", target_bir_lowering=False, num_devices=NCORE)

    x2T = nc.dram_tensor("x2T", [D, TOK], F32, kind="ExternalInput").ap()
    qb_i = nc.dram_tensor("qb_i", [H_A, HD, TOK], BF16, kind="ExternalInput").ap()
    kb_own = nc.dram_tensor("kb_own", [HKV, HD, TOK], BF16, kind="ExternalInput").ap()
    vb_own = nc.dram_tensor("vb_own", [HKV, TOK, HD], BF16, kind="ExternalInput").ap()
    kb_sl = nc.dram_tensor("kb_sl", [HKV, 7, HD, CHUNK], BF16, kind="ExternalInput").ap()
    vb_sl = nc.dram_tensor("vb_sl", [HKV, 7, CHUNK, HD], BF16, kind="ExternalInput").ap()
    vbT_i = nc.dram_tensor("vbT_i", [HKV, HD, TOK], BF16, kind="ExternalInput").ap()
    uw_i = nc.dram_tensor("uw_i", [2, 128, 9], F32, kind="ExternalInput").ap()
    masks_i = nc.dram_tensor("masks_i", [128, 8, 2, 512], F32, kind="ExternalInput").ap()
    wp_i = nc.dram_tensor("wp_i", [D, D], F32, kind="ExternalInput").ap()
    pA_i = nc.dram_tensor("pA_i", [D, RANK], F32, kind="ExternalInput").ap()
    pB_i = nc.dram_tensor("pB_i", [RANK, D], F32, kind="ExternalInput").ap()
    qgain_i = nc.dram_tensor("qgain_i", [1, H_A], F32, kind="ExternalInput").ap()
    wg_i = nc.dram_tensor("wg_i", [D, INTER], BF16, kind="ExternalInput").ap()
    wu_i = nc.dram_tensor("wu_i", [D, INTER], BF16, kind="ExternalInput").ap()
    wd_i = nc.dram_tensor("wd_i", [INTER, D], BF16, kind="ExternalInput").ap()
    ones_i = nc.dram_tensor("ones_row", [1, 128], F32, kind="ExternalInput").ap()
    invd_i = nc.dram_tensor("invd_col_i", [128, 1], F32, kind="ExternalInput").ap()
    onesc_i = nc.dram_tensor("onesc_i", [128, 1], F32, kind="ExternalInput").ap()
    consts_i = nc.dram_tensor("consts_i", [128, 8], F32, kind="ExternalInput").ap()

    outT = nc.dram_tensor("outT", [D, TOK], F32, kind="ExternalOutput").ap()
    resid_d = nc.dram_tensor("resid_d", [D, TOK], F32)

    with ExitStack() as ctx:
        e = ctx.enter_context
        invd_col = e(nc.sbuf_tensor("invd_col", [128, 1], BF16))
        onesb_col = e(nc.sbuf_tensor("onesb_col", [128, 1], BF16))
        ones_f32 = e(nc.sbuf_tensor("ones_f32", [1, 128], F32))
        const_sb = e(nc.sbuf_tensor("const_sb", [128, 8], F32))
        uw_sb = e(nc.sbuf_tensor("uw_sb", [128, 2, 9], F32))
        masks_sb = e(nc.sbuf_tensor("masks_sb", [128, 8, 2, 512], BF16))
        qg_sb = e(nc.sbuf_tensor("qg_sb", [1, H_A], F32))

        block = e(nc.Block())
        p = Prog(nc, ctx)
        pe, dve, act = p.ch("pe"), p.ch("dve"), p.ch("act")
        gc, sc = p.ch("gconst"), p.ch("sconst")

        C = dict(invd_col=invd_col[:, :], ones_f32=ones_f32[:, :],
                 eps1=const_sb[0:1, 1:2], zero1=const_sb[0:1, 0:1],
                 zero128=const_sb[:, 0:1],
                 epsr1=const_sb[0:1, 2:3], tiny1=const_sb[0:1, 4:5])

        p.dma("gpsimd", gc, invd_col[:, :], invd_i)
        p.dma("gpsimd", gc, onesb_col[:, :], onesc_i)
        p.dma("gpsimd", gc, masks_sb[:, :, :, :], masks_i)
        p.dma("sync", sc, ones_f32[:, :], ones_i)
        p.dma("sync", sc, const_sb[:, :], consts_i)
        p.dma("sync", sc, uw_sb[:, 0, :], uw_i[0, :, :])
        p.dma("sync", sc, uw_sb[:, 1, :], uw_i[1, :, :])
        p.dma("sync", sc, qg_sb[:, :], qgain_i)
        for eng in ("tensor", "vector", "scalar"):
            p.wait(eng, gc, gc.n)
            p.wait(eng, sc, sc.n)

        with ExitStack() as s1:
            e1 = s1.enter_context
            z_buf = e1(nc.sbuf_tensor("z_buf", [128, H_A, TOK], BF16))
            qa_sb = e1(nc.sbuf_tensor("qa_sb", [128, 2, TOK], BF16))
            kk_sb = e1(nc.sbuf_tensor("kk_sb", [128, 2, CHUNK], BF16))
            vv_sb = e1(nc.sbuf_tensor("vv_sb", [128, 2, 8, 128], BF16))
            qu_sb = e1(nc.sbuf_tensor("qu_sb", [128, CHUNK], BF16))
            pt_sb = e1(nc.sbuf_tensor("pt_sb", [128, 2, CHUNK], BF16))
            yA = e1(nc.sbuf_tensor("yA", [128, CHUNK], F32))
            yB = e1(nc.sbuf_tensor("yB", [128, CHUNK], F32))
            dAr = e1(nc.sbuf_tensor("dAr", [1, CHUNK], F32))
            dBr = e1(nc.sbuf_tensor("dBr", [1, CHUNK], F32))
            rrow = e1(nc.sbuf_tensor("rrow", [1, 2, CHUNK], F32))
            bias_sb = e1(nc.sbuf_tensor("bias_sb", [128, H_A], F32))
            grow = e1(nc.sbuf_tensor("grow", [1, H_A], F32))
            vbT_sb = e1(nc.sbuf_tensor("vbT_sb", [128, HKV, TOK], BF16))
            vn_sb = e1(nc.sbuf_tensor("vn_sb", [128, HKV, TOK], BF16))
            sqv_sb = e1(nc.sbuf_tensor("sqv_sb", [128, TOK], BF16))
            vrow = e1(nc.sbuf_tensor("vrow", [1, TOK], F32))
            tdot = e1(nc.sbuf_tensor("tdot", [128, TOK], F32))
            st_ps = e1(nc.psum_tensor("st_ps", [128, 2, 2, 512], F32))
            y_ps = e1(nc.psum_tensor("y_ps", [128, 2, 512], F32))
            dr_ps = e1(nc.psum_tensor("dr_ps", [128, 2, 512], F32))

            vbc = p.ch("vbc")
            p.dma("gpsimd", vbc, vbT_sb[:, :, :], vbT_i.rearrange("k h t -> h k t"))
            p.wait("vector", vbc, vbc.n)
            p.wait("tensor", vbc, vbc.n)

            p.op("vector", lambda e: e.tensor_scalar(grow[:, :], qg_sb[:, :], -1.0, None, ALU.mult))
            p.op("vector", lambda e: e.tensor_tensor(grow[:, :], grow[:, :], qg_sb[:, :], ALU.max))
            p.op("vector", lambda e: e.tensor_scalar(grow[:, :], grow[:, :], -11.313708498984761, None, ALU.mult), inc=dve)
            p.wait("tensor", dve, dve.n)
            for h in range(H_A):
                p.wait("tensor", act, act.n)
                p.op("tensor", lambda e, h=h: e.matmul(st_ps[:, 0, 0, 0:1], ones_f32[:, :],
                                                       grow[:, h:h + 1],
                                                       start=True, stop=True), inc=pe)
                p.wait("scalar", pe, pe.n)
                p.op("scalar", lambda e, h=h: e.copy(bias_sb[:, h:h + 1], st_ps[:, 0, 0, 0:1]), inc=act)
            p.wait("tensor", act, act.n)

            for kv in range(HKV):
                p.op("vector", lambda e, kv=kv: e.tensor_tensor(
                    sqv_sb[:, :], vbT_sb[:, kv, :], vbT_sb[:, kv, :], ALU.mult), inc=dve)
                p.wait("tensor", dve, dve.n)
                for jp in range(2):
                    p.wait("tensor", act, act.n)
                    for j in (2 * jp, 2 * jp + 1):
                        p.op("tensor", lambda e, j=j: e.matmul(
                            dr_ps[0:1, j % 2, :], onesb_col[:, :], sqv_sb[:, j * 512:(j + 1) * 512],
                            start=True, stop=True), inc=(pe if j % 2 == 1 else None))
                    p.wait("scalar", pe, pe.n)
                    p.op("scalar", lambda e, jp=jp: e.copy(vrow[:, jp * 1024:(jp + 1) * 1024],
                                                           dr_ps[0:1, :, :].rearrange("p a b -> p (a b)")), inc=act)
                p.wait("scalar", act, act.n)
                p.op("scalar", lambda e: e.activation(vrow[:, :], vrow[:, :], AF.Ln, bias=C["tiny1"], scale=1.0), inc=act)
                p.op("scalar", lambda e: e.activation(vrow[:, :], vrow[:, :], AF.Exp, bias=C["zero1"], scale=-0.5), inc=act)
                p.wait("tensor", act, act.n)
                for j in range(4):
                    p.wait("tensor", dve, dve.n)
                    p.op("tensor", lambda e, j=j: e.matmul(
                        st_ps[:, 0, j % 2, :], ones_f32[:, :], vrow[:, j * 512:(j + 1) * 512],
                        start=True, stop=True), inc=pe)
                    p.wait("vector", pe, pe.n)
                    p.op("vector", lambda e, kv=kv, j=j: e.tensor_tensor(
                        vn_sb[:, kv, j * 512:(j + 1) * 512], vbT_sb[:, kv, j * 512:(j + 1) * 512],
                        st_ps[:, 0, j % 2, :], ALU.mult), inc=dve)
                p.wait("tensor", dve, dve.n)

            qch = [p.ch("qb0"), p.ch("qb1")]
            kch = [p.ch("kb0"), p.ch("kb1")]
            vch = [p.ch("vb0"), p.ch("vb1")]
            A = dict(qa_sb=qa_sb, kk_sb=kk_sb, vv_sb=vv_sb, qu_sb=qu_sb,
                     pt_sb=pt_sb, yA=yA, yB=yB, dA=dAr, dB=dBr, rrow=rrow,
                     st_ps=st_ps, y_ps=y_ps, den_ps=dr_ps[0:1, :, :], rc_ps=dr_ps,
                     masks=masks_sb, z_buf=z_buf,
                     uwA=uw_sb[:, 0, :], uwB=uw_sb[:, 1, :],
                     onesb=onesb_col[:, :], ones_f32=ones_f32[:, :],
                     q_ch=qch, q_n=[0, 0], kv_chk=kch, kv_chv=vch,
                     kv_nk=[0, 0], kv_nv=[0, 0],
                     q_eng="sync", kv_eng="gpsimd",
                     zmap=lambda h: h,
                     bias_col=lambda h: bias_sb[:, h:h + 1])

            def load_q(h, slot):
                p.dma("sync", qch[slot], qa_sb[:, slot, :], qb_i[h, :, :])
                A["q_n"][slot] = qch[slot].n
            A["load_q"] = load_q

            def load_kv(h, u, slot):
                kv = h // 4
                if u == 0:
                    ksrc, vsrc = kb_own[kv, :, 0:CHUNK], vb_own[kv, 0:CHUNK, :]
                elif u == 1:
                    ksrc, vsrc = kb_own[kv, :, CHUNK:TOK], vb_own[kv, CHUNK:TOK, :]
                else:
                    ksrc, vsrc = kb_sl[kv, u - 2, :, :], vb_sl[kv, u - 2, :, :]
                p.dma("gpsimd", kch[slot], kk_sb[:, slot, :], ksrc)
                A["kv_nk"][slot] = kch[slot].n
                p.dma("gpsimd", vch[slot], vv_sb[:, slot, :, :],
                      vsrc.rearrange("(c p) d -> p c d", p=128))
                A["kv_nv"][slot] = vch[slot].n
            A["load_kv"] = load_kv

            z_done = emit_attn_core(p, A, H_A, 0.08838834764831845)

            p.wait("vector", dve, z_done)
            for h in range(H_A):
                kv = h // 4
                p.op("vector", lambda e, h=h, kv=kv: e.tensor_tensor(
                    tdot[:, :], z_buf[:, h, :], vn_sb[:, kv, :], ALU.mult), inc=dve)
                p.op("vector", lambda e: e.tensor_copy(sqv_sb[:, :], tdot[:, :]), inc=dve)
                p.wait("tensor", dve, dve.n)
                for jp in range(2):
                    p.wait("tensor", act, act.n)
                    for j in (2 * jp, 2 * jp + 1):
                        p.op("tensor", lambda e, j=j: e.matmul(
                            dr_ps[0:1, j % 2, :], onesb_col[:, :], sqv_sb[:, j * 512:(j + 1) * 512],
                            start=True, stop=True), inc=(pe if j % 2 == 1 else None))
                    p.wait("scalar", pe, pe.n)
                    p.op("scalar", lambda e, jp=jp: e.copy(vrow[:, jp * 1024:(jp + 1) * 1024],
                                                           dr_ps[0:1, :, :].rearrange("p a b -> p (a b)")), inc=act)
                p.wait("tensor", act, act.n)
                for j in range(4):
                    p.wait("tensor", dve, dve.n)
                    p.op("tensor", lambda e, j=j: e.matmul(
                        st_ps[:, 0, j % 2, :], ones_f32[:, :], vrow[:, j * 512:(j + 1) * 512],
                        start=True, stop=True), inc=pe)
                    p.wait("vector", pe, pe.n)
                    p.op("vector", lambda e, kv=kv, j=j: e.tensor_tensor(
                        tdot[:, j * 512:(j + 1) * 512], st_ps[:, 0, j % 2, :],
                        vn_sb[:, kv, j * 512:(j + 1) * 512], ALU.mult), inc=dve)
                p.op("vector", lambda e, h=h: e.tensor_tensor(
                    z_buf[:, h, :], z_buf[:, h, :], tdot[:, :], ALU.subtract), inc=dve)
                p.wait("tensor", dve, dve.n)

            woslab = e1(nc.sbuf_tensor("woslab", [128, 2, DT_PER, 128], BF16))
            adA_sb = e1(nc.sbuf_tensor("adA_sb", [128, DT_PER, RANK], BF16))
            adB_sb = e1(nc.sbuf_tensor("adB_sb", [16, D], BF16))
            uad_sb = e1(nc.sbuf_tensor("uad_sb", [16, TB], BF16))
            xin1 = e1(nc.sbuf_tensor("xin1", [128, TB], F32))
            wp_t = wp_i.rearrange("(ft p) m -> p ft m", p=128)
            x2_t = x2T.rearrange("(dt p) t -> p dt t", p=128)
            rd_t = resid_d.rearrange("(dt p) t -> p dt t", p=128)
            adc = p.ch("adc")
            p.dma("gpsimd", adc, adA_sb[:, :, :], pA_i.rearrange("(dt p) r -> p dt r", p=128))
            p.dma("gpsimd", adc, adB_sb[:, :], pB_i)
            p.wait("tensor", adc, adc.n)
            wo_ch = [p.ch("wo0"), p.ch("wo1")]
            x1_ch = p.ch("x1c")
            xi_ch = p.ch("xic")
            wo_free = [0, 0]
            uad_all = e1(nc.sbuf_tensor("uad_all", [16, TOK], BF16))
            for tb in range(NTB):
                p.wait("tensor", act, act.n)
                for ft in range(DT_PER):
                    p.op("tensor", lambda e, ft=ft, tb=tb: e.matmul(
                        y_ps[0:RANK, 0, :], adA_sb[:, ft, :],
                        z_buf[:, ft, tb * TB:(tb + 1) * TB],
                        start=(ft == 0), stop=(ft == DT_PER - 1)),
                        inc=(pe if ft == DT_PER - 1 else None))
                p.wait("scalar", pe, pe.n)
                p.op("scalar", lambda e, tb=tb: e.copy(uad_all[:, tb * TB:(tb + 1) * TB],
                                                       y_ps[0:RANK, 0, :]), inc=act)
            uad_done = act.n
            p.wait("tensor", act, uad_done)
            ps_free = [0, 0]
            for dt in range(DT_PER):
                wsl = dt % 2
                if wo_free[wsl]:
                    p.wait("gpsimd", pe, wo_free[wsl])
                p.dma("gpsimd", wo_ch[wsl], woslab[:, wsl, :, :], wp_t[:, :, dt * 128:(dt + 1) * 128])
                p.wait("tensor", wo_ch[wsl], wo_ch[wsl].n)
                for tb in range(NTB):
                    psl = tb % 2
                    if ps_free[psl]:
                        p.wait("tensor", dve, ps_free[psl])
                    for ft in range(DT_PER):
                        p.op("tensor", lambda e, wsl=wsl, psl=psl, ft=ft, tb=tb: e.matmul(
                            st_ps[:, psl, 0, :], woslab[:, wsl, ft, :],
                            z_buf[:, ft, tb * TB:(tb + 1) * TB],
                            start=(ft == 0), stop=False))
                    p.op("tensor", lambda e, wsl=wsl, psl=psl, dt=dt, tb=tb: e.matmul(
                        st_ps[:, psl, 0, :], adB_sb[:, dt * 128:(dt + 1) * 128],
                        uad_all[:, tb * TB:(tb + 1) * TB],
                        start=False, stop=True), inc=pe)
                    wo_mm = pe.n
                    p.wait("sync", dve, dve.n)
                    p.wait("sync", x1_ch, x1_ch.n)
                    p.dma("sync", xi_ch, xin1[:, :], x2_t[:, dt, tb * TB:(tb + 1) * TB])
                    p.wait("vector", xi_ch, xi_ch.n)
                    p.wait("vector", pe, wo_mm)
                    p.op("vector", lambda e, psl=psl: e.tensor_tensor(
                        xin1[:, :], xin1[:, :], st_ps[:, psl, 0, :], ALU.add), inc=dve)
                    ps_free[psl] = dve.n
                    p.wait("sync", dve, dve.n)
                    p.dma("sync", x1_ch, rd_t[:, dt, tb * TB:(tb + 1) * TB], xin1[:, :])
                wo_free[wsl] = pe.n

        _barrier(p)

        with ExitStack() as s2:
            e2 = s2.enter_context
            at_sb = e2(nc.sbuf_tensor("at_sb", [128, IT_PER, TB], BF16))
            gslab = e2(nc.sbuf_tensor("gslab", [128, 2, 2, DT_PER, 128], BF16))
            dslab = e2(nc.sbuf_tensor("dslab", [128, 2, IT_PER, 128], BF16))
            xin_sb = e2(nc.sbuf_tensor("xin_sb", [128, DT_PER, TB], F32))
            sil_sb = e2(nc.sbuf_tensor("sil_sb", [128, TB], BF16))
            gu_ps = e2(nc.psum_tensor("gu_ps", [128, 2, TB], F32))
            dn_ps = e2(nc.psum_tensor("dn_ps", [128, 2, TB], F32))
            psA2 = e2(nc.psum_tensor("psA2", [128, 2 * TB], F32))
            bcA2 = e2(nc.psum_tensor("bcA2", [128, 2, TB], F32))
            C["psA"] = psA2
            C["bcA"] = bcA2
            C["xcast"] = e2(nc.sbuf_tensor("xcast", [128, DT_PER, TB], BF16))
            C["xsq"] = e2(nc.sbuf_tensor("xsq", [128, DT_PER, TB], BF16))
            C["rows4"] = e2(nc.sbuf_tensor("rows4", [1, 4, TB], F32))
            C["ln_t"] = e2(nc.sbuf_tensor("ln_t", [128, TB], F32))
            Fd = dict(wg=wg_i, wu=wu_i, wd=wd_i,
                      gslab=gslab, dslab=dslab, at_sb=at_sb,
                      xin_sb=xin_sb, xout_sb=xin_sb,
                      gu_ps=gu_ps, dn_ps=dn_ps, sil_sb=sil_sb,
                      ch_g=[p.ch("ffg0"), p.ch("ffg1")],
                      ch_d=[p.ch("ffd0"), p.ch("ffd1")],
                      ch_xi=p.ch("ffxi"), ch_xo=p.ch("ffxo"),
                      g_eng="gpsimd", d_eng="gpsimd", xi_eng="sync", xo_eng="sync")
            emit_ffn(p, Fd, resid_d, outT, C)

        for chn in list(p.chs.values()):
            if chn.n and chn.name not in ("pe", "dve", "act", "gp"):
                p.wait("gpsimd", chn, chn.n)
        p.finalize(block)
    return nc


def run_L3(inputs, res2):
    if "L3" not in _build_cache:
        _build_cache["L3"] = build_L3()
    nc3 = _build_cache["L3"]
    r = res2.results
    pubK = [r[c]["kb_pub"] for c in range(NCORE)]
    pubV = [r[c]["vb_pub"] for c in range(NCORE)]
    cvals = host_consts_vals()
    hc0 = host_constants(0)
    wg_full = _build_cache["wg_full"]
    wu_full = _build_cache["wu_full"]
    wd_full = _build_cache["wd_full"]

    in_maps = []
    for c in range(NCORE):
        wA, wB, _ = attn_unit_table(c)
        kk, vv = host_kv_slots(c, pubK, pubV, HKV)
        in_maps.append(dict(
            x2T=r[c]["x2T"],
            qb_i=r[c]["qb_pub"], kb_own=r[c]["kb_pub"], vb_own=r[c]["vb_pub"],
            kb_sl=kk, vb_sl=vv, vbT_i=r[c]["vbT_own"],
            uw_i=np.stack([np.broadcast_to(wA, (128, 9)),
                           np.broadcast_to(wB, (128, 9))]).astype(np.float32),
            masks_i=host_diag_masks(),
            wp_i=_f32(inputs["L0_attn_b_Wproj"]),
            pA_i=_f32(inputs["L0_attn_b_p_A"]), pB_i=_f32(inputs["L0_attn_b_p_B"]),
            qgain_i=_f32(inputs["L0_attn_b_q_gain"]).reshape(1, H_A),
            wg_i=wg_full, wu_i=wu_full, wd_i=wd_full,
            ones_row=hc0["ones_row"], invd_col_i=hc0["invd_col"],
            onesc_i=np.ones((128, 1), np.float32),
            consts_i=cvals,
        ))
    return run_bass_kernel_spmd(nc3, in_maps, core_ids=list(range(NCORE)))
